# version 15
# baseline (speedup 1.0000x reference)
import contextlib
import numpy as np

import concourse.bass as bass
import concourse.mybir as mybir
from concourse.bass_utils import run_bass_kernel_spmd

F32 = mybir.dt.float32
BF16 = mybir.dt.bfloat16
AF = mybir.ActivationFunctionType
ALU = mybir.AluOpType
AX = mybir.AxisListType

NCORES = 8
SEQ = 2048
BLOC = 2
T = SEQ * BLOC
D = 1024
H = 4096
NST = 16
DCONV = 4
LN_EPS = 1e-5
G = D // 128
HT = H // 128
TC = 512
NCH = T // TC
NSUB = TC // 128
RCONST = float(3 << 22)
DEBUG_TAPS = False


SEMKEYS = ("sy", "gp", "te", "ve", "sc", "params", "x", "xg", "pf", "dc", "dcg",
           "st0", "st1", "st2", "st3",
           "gs0", "gs1", "os0", "os1", "w10", "w11", "w20", "w21",
           "wp0", "wp1", "scr0", "scr1")


class Ledger:
    def __init__(self, prev_marks):
        self.cnt = {k: 0 for k in SEMKEYS}
        self.marks = {}
        self.prev = prev_marks

    def tick(self, sems, key, inst, k=1):
        inst.then_inc(sems[key], k)
        self.cnt[key] += k

    def mark(self, name, key):
        assert name not in self.marks, name
        self.marks[name] = (key, self.cnt[key])

    def wait(self, sems, eng, name):
        if self.prev is None:
            return
        key, val = self.prev[name]
        if val > 0:
            eng.wait_ge(sems[key], val)


def emit(nc, led):
    L = led

    dx = nc.dram_tensor("x", [T, D], F32, kind="ExternalInput").ap()
    dconv_w = nc.dram_tensor("conv_w", [D, DCONV], F32, kind="ExternalInput").ap()
    dconv_b = nc.dram_tensor("conv_b", [D], F32, kind="ExternalInput").ap()
    dA = nc.dram_tensor("A", [D, NST], F32, kind="ExternalInput").ap()
    dBpw = nc.dram_tensor("Bp_w", [NST, D], F32, kind="ExternalInput").ap()
    dBpb = nc.dram_tensor("Bp_b", [NST], F32, kind="ExternalInput").ap()
    dCpw = nc.dram_tensor("Cp_w", [NST, D], F32, kind="ExternalInput").ap()
    dCpb = nc.dram_tensor("Cp_b", [NST], F32, kind="ExternalInput").ap()
    dD = nc.dram_tensor("Dv", [D], F32, kind="ExternalInput").ap()
    dgw = nc.dram_tensor("gate_w", [D, D], F32, kind="ExternalInput").ap()
    dgb = nc.dram_tensor("gate_b", [D], F32, kind="ExternalInput").ap()
    dow = nc.dram_tensor("out_w", [D, D], F32, kind="ExternalInput").ap()
    dob = nc.dram_tensor("out_b", [D], F32, kind="ExternalInput").ap()
    dlng = nc.dram_tensor("ln_g", [D], F32, kind="ExternalInput").ap()
    dlnb = nc.dram_tensor("ln_b", [D], F32, kind="ExternalInput").ap()
    dw1 = nc.dram_tensor("w1", [H, D], F32, kind="ExternalInput").ap()
    db1 = nc.dram_tensor("b1", [H], F32, kind="ExternalInput").ap()
    dw2 = nc.dram_tensor("w2", [D, H], F32, kind="ExternalInput").ap()
    db2 = nc.dram_tensor("b2", [D], F32, kind="ExternalInput").ap()
    dout = nc.dram_tensor("out", [T, D], F32, kind="ExternalOutput").ap()
    scr_w1 = nc.dram_tensor("scr_w1", [HT, 128, 1024], BF16, kind="Internal").ap()
    scr_w2 = nc.dram_tensor("scr_w2", [G, 128, 4096], BF16, kind="Internal").ap()
    scr_g = nc.dram_tensor("scr_g", [G, 128, 1024], BF16, kind="Internal").ap()
    scr_o = nc.dram_tensor("scr_o", [G, 128, 1024], BF16, kind="Internal").ap()
    scr_dc = nc.dram_tensor("scr_dc", [128, G * DCONV * 128], BF16,
                            kind="Internal").ap()
    dbg = {}
    if DEBUG_TAPS:
        for nm, shape, dt in (
                ("xn16", [128, G * (3 + TC)], BF16),
                ("u16d", [128, G * TC], BF16),
                ("btd", [NST, 2 * TC], BF16),
                ("BCd", [128, 2 * NST * TC], BF16),
                ("xqd", [128, G * TC], BF16),
                ("hqd", [128, HT * TC], BF16),
                ("gad", [128, G * TC], BF16),
                ("zresd", [128, G * TC], F32),
                ("carryd", [128, G * NST], F32),
                ("acold", [128, G * NST], F32),
                ("t1d", [128, 3 * TC], F32),
                ("statd", [128, NSUB * 8], F32),
                ("xtmd", [128, NSUB * D], F32),
                ("scrw1d", [128, 1024], BF16),
                ("scrw2d", [128, 4096], BF16)):
            dbg[nm] = nc.dram_tensor("dbg_" + nm, shape, dt,
                                     kind="ExternalOutput").ap()

    ctx = contextlib.ExitStack()
    sb = lambda name, shape, dt=F32: ctx.enter_context(
        nc.sbuf_tensor(name, shape, dt))
    ps = lambda name: ctx.enter_context(nc.psum_tensor(name, [128, 512], F32))

    with ctx:
        s_BpCpT = sb("BpCpT", [128, 2 * G * NST], BF16)
        s_acol = sb("acol", [128, G * NST])
        s_diagD = sb("diagD", [128, G * 128], BF16)
        s_convw = sb("convw", [128, G * DCONV])
        s_cols = sb("cols", [128, 5 * G])
        OF_CONVB, OF_GATEB, OF_OUTB, OF_LNG, OF_LNB2 = (
            0, G, 2 * G, 3 * G, 4 * G)
        s_b1qC = sb("b1qC", [128, HT])
        s_sel = sb("sel", [NST, NST * 128], BF16)
        s_bpb = sb("bpb", [NST, 2])
        s_id32 = sb("id32", [128, 128])
        s_id16 = sb("id16", [128, 128], BF16)
        s_ones1 = sb("ones1", [1, 128])
        s_negC = sb("negC", [128, 1])
        s_ones16 = sb("ones16c", [128, 1], BF16)
        s_carry = sb("carry", [128, G * NST])
        s_w1s = sb("w1s", [128, 2 * 1024], BF16)
        s_w2s = sb("w2s", [128, 2 * 4096], BF16)
        s_gs = sb("gs", [128, 2 * 1024], BF16)
        s_os = sb("osb", [128, 2 * 1024], BF16)
        s_xtm = sb("xtm", [128, NSUB * D])
        s_stat = sb("stat", [128, NSUB * 8])
        s_xq16 = sb("xq16", [128, D], BF16)
        s_xqtm = sb("xqtm", [128, D])
        s_fstg = sb("fstg", [128, 512])
        s_xn16 = sb("xn16", [128, 2 * G * (3 + TC)], BF16)
        s_xq = sb("xq", [128, 2 * G * TC], BF16)
        s_u16 = sb("u16", [128, G * TC], BF16)
        s_bt = sb("bt", [NST, 2 * TC], BF16)
        s_BC = sb("BC", [128, 2 * NST * TC], BF16)
        s_w = sb("wbuf", [128, TC], BF16)
        s_s = sb("sbuf_s", [128, 2 * TC])
        s_z = sb("zbuf", [128, TC], BF16)
        s_ga = sb("ga", [128, G * TC], BF16)
        s_zres = sb("zres", [128, G * TC])
        s_z16 = sb("z16", [128, 2 * TC], BF16)
        s_hq = sb("hq", [128, HT * TC], BF16)
        s_t1 = sb("t1", [128, 3 * TC])
        pTA, pTB = ps("pTA"), ps("pTB")
        pMA, pMB = ps("pMA"), ps("pMB")
        pBCa, pBCb = ps("pBCa"), ps("pBCb")
        pYA, pYB = ps("pYA"), ps("pYB")

        sems = {k: ctx.enter_context(nc.semaphore("sem_" + k))
                for k in SEMKEYS}
        block = ctx.enter_context(nc.Block())

        def tick(key, inst, k=1):
            L.tick(sems, key, inst, k)

        def mark(name, key):
            L.mark(name, key)

        def wt(eng, name):
            L.wait(sems, eng, name)

        import contextlib as _cl

        def scoped(name):
            return nc.named_scope(name)

        @block.gpsimd
        def _(gp):
            def ss(inst):
                tick("gp", inst)
                gp.wait_ge(sems["gp"], L.cnt["gp"])

            ss(gp.memset(s_id32[:], 0.0))
            ss(nc.gpsimd.affine_select(
                out=s_id32[:], in_=s_id32[:], compare_op=ALU.not_equal,
                fill=1.0, base=0, pattern=[[-1, 128]], channel_multiplier=1))
            ss(gp.memset(s_id16[:], 0.0))
            ss(nc.gpsimd.affine_select(
                out=s_id16[:], in_=s_id16[:], compare_op=ALU.not_equal,
                fill=1.0, base=0, pattern=[[-1, 128]], channel_multiplier=1))
            ss(gp.memset(s_sel[:], 0.0))
            sel3 = s_sel.rearrange("p (n f) -> p n f", f=128)
            ss(nc.gpsimd.affine_select(
                out=sel3, in_=sel3, compare_op=ALU.not_equal,
                fill=1.0, base=0, pattern=[[-1, NST], [0, 128]],
                channel_multiplier=1))
            gp.memset(s_ones1[:], 1.0)
            gp.memset(s_negC[:], -RCONST)
            gp.memset(s_ones16[:], 1.0)
            gp.memset(s_carry[:], 0.0)
            tick("gp", gp.memset(s_xn16[:], 0.0))
            mark("gp_const", "gp")

        @block.sync
        def _(sy):
            for g in range(G):
                tick("params", sy.dma_start(
                    out=s_convw[:, g * DCONV:(g + 1) * DCONV],
                    in_=dconv_w[g * 128:(g + 1) * 128, :]), 16)
                tick("params", sy.dma_start(
                    out=s_cols[:, OF_CONVB + g:OF_CONVB + g + 1],
                    in_=dconv_b[g * 128:(g + 1) * 128][:, None]), 16)
                tick("params", sy.dma_start(
                    out=s_cols[:, OF_GATEB + g:OF_GATEB + g + 1],
                    in_=dgb[g * 128:(g + 1) * 128][:, None]), 16)
                tick("params", sy.dma_start(
                    out=s_cols[:, OF_OUTB + g:OF_OUTB + g + 1],
                    in_=dob[g * 128:(g + 1) * 128][:, None]), 16)
                tick("params", sy.dma_start(
                    out=s_cols[:, OF_LNG + g:OF_LNG + g + 1],
                    in_=dlng[g * 128:(g + 1) * 128][:, None]), 16)
                tick("params", sy.dma_start(
                    out=s_acol[:, g * NST:(g + 1) * NST],
                    in_=dA[g * 128:(g + 1) * 128, :]), 16)
                tick("params", sy.dma_start(
                    out=s_stat[:, g:g + 1],
                    in_=dD[g * 128:(g + 1) * 128][:, None]), 16)
                tick("params", sy.dma_start(
                    out=s_stat[:, 8 + g:8 + g + 1],
                    in_=dlnb[g * 128:(g + 1) * 128][:, None]), 16)
            tick("params", sy.dma_start(out=s_bpb[:, 0:1], in_=dBpb[:, None]), 16)
            tick("params", sy.dma_start(out=s_bpb[:, 1:2], in_=dCpb[:, None]), 16)
            tick("params", sy.dma_start(
                out=s_zres[0:32, 0:128],
                in_=db1.rearrange("(a b) -> a b", b=128)), 16)
            tick("params", sy.dma_start(
                out=s_zres[0:8, 200:328],
                in_=db2.rearrange("(a b) -> a b", b=128)), 16)
            tick("params", sy.dma_start(
                out=s_zres[0:NST, 400:400 + D], in_=dBpw[:, :]), 16)
            tick("params", sy.dma_start(
                out=s_zres[0:NST, 1500:1500 + D], in_=dCpw[:, :]), 16)
            mark("sy_params", "params")

        @block.scalar
        def _(sc):
            wt(sc, "sy_params")
            tick("sc", sc.activation(s_acol[:], s_acol[:], AF.Exp))
            sc.wait_ge(sems["sc"], L.cnt["sc"])
            tick("sc", sc.activation(s_acol[:], s_acol[:], AF.Exp, scale=-1.0))
            mark("sc_acol", "sc")

        @block.tensor
        def _(te):
            wt(te, "sy_params")
            wt(te, "gp_const")
            te.transpose(pTA[:, 0:32], s_zres[0:32, 0:128],
                         s_id32[0:32, 0:32])
            te.transpose(pTA[:, 32:40], s_zres[0:8, 200:328],
                         s_id32[0:8, 0:8])
            for g in range(G):
                te.transpose(pTB[:, g * 16:(g + 1) * 16],
                             s_zres[0:NST, 400 + g * 128:400 + (g + 1) * 128],
                             s_id32[0:NST, 0:NST])
            for g in range(G):
                _last = te.transpose(pTB[:, 128 + g * 16:128 + (g + 1) * 16],
                             s_zres[0:NST, 1500 + g * 128:1500 + (g + 1) * 128],
                             s_id32[0:NST, 0:NST])
            tick("te", _last)
            mark("te_prep1", "te")

        @block.vector
        def _(ve):
            def vst(inst):
                tick("ve", inst)
                ve.wait_ge(sems["ve"], L.cnt["ve"])

            wt(ve, "te_prep1")
            wt(ve, "sy_params")
            wt(ve, "gp_const")
            for g in range(G):
                ve.tensor_scalar(s_diagD[:, g * 128:(g + 1) * 128],
                                 s_id16[:], s_stat[:, g:g + 1], None, ALU.mult)
            for g in range(G):
                for k in range(DCONV):
                    i = g * DCONV + k
                    _last = ve.tensor_scalar(
                        s_BC[:, i * 128:(i + 1) * 128],
                        s_id16[:], s_convw[:, i:i + 1], None, ALU.mult)
            tick("ve", _last)
            mark("dconv_stage", "ve")
            vst(ve.tensor_scalar(s_b1qC[:], pTA[:, 0:HT], 64.0, RCONST,
                                 ALU.mult, ALU.add))
            vst(ve.tensor_scalar(s_b1qC[:], s_b1qC[:], RCONST + 127.0,
                                 RCONST - 128.0, ALU.min, ALU.max))
            vst(ve.tensor_scalar(s_b1qC[:], s_b1qC[:], RCONST, None,
                                 ALU.subtract))
            vst(ve.tensor_scalar(s_b1qC[:], s_b1qC[:], 0.25, None, ALU.mult))
            vst(ve.tensor_scalar(s_cols[:, OF_LNB2:OF_LNB2 + 8],
                                 pTA[:, 32:40], 64.0, RCONST,
                                 ALU.mult, ALU.add))
            vst(ve.tensor_scalar(s_cols[:, OF_LNB2:OF_LNB2 + 8],
                                 s_cols[:, OF_LNB2:OF_LNB2 + 8],
                                 RCONST + 127.0, RCONST - 128.0,
                                 ALU.min, ALU.max))
            vst(ve.tensor_scalar(s_cols[:, OF_LNB2:OF_LNB2 + 8],
                                 s_cols[:, OF_LNB2:OF_LNB2 + 8],
                                 RCONST, None, ALU.subtract))
            vst(ve.tensor_scalar(s_stat[:, 8:16], s_stat[:, 8:16],
                                 1024.0, None, ALU.mult))
            vst(ve.scalar_tensor_tensor(
                s_cols[:, OF_LNB2:OF_LNB2 + 8],
                s_cols[:, OF_LNB2:OF_LNB2 + 8], 16.0,
                s_stat[:, 8:16], ALU.mult, ALU.add))
            vst(ve.tensor_scalar(s_cols[:, OF_LNG:OF_LNG + 8],
                                 s_cols[:, OF_LNG:OF_LNG + 8],
                                 1024.0, None, ALU.mult))
            ve.tensor_copy(s_BpCpT[:, 0:G * NST], pTB[:, 0:G * NST])
            tick("ve", ve.tensor_copy(s_BpCpT[:, G * NST:2 * G * NST],
                                      pTB[:, 128:128 + G * NST]))
            mark("ve_prep1", "ve")

        wjobs = []
        for (src, dst, nrt, rowlen, mode) in (
                (dw1, scr_w1, HT, D, "q64"),
                (dw2, scr_w2, G, H, "q64"),
                (dgw, scr_g, G, D, "cast"),
                (dow, scr_o, G, D, "cast")):
            for j in range(nrt):
                wjobs.append((src, dst, j, rowlen, mode))
        NJ = len(wjobs)

        def wp_in(i, rowlen):
            if rowlen == H:
                return s_zres[:, 0:H]
            return s_xtm[:, (i % 2) * 2048:(i % 2) * 2048 + rowlen]

        def wp_t1(i, rowlen):
            if rowlen == H:
                return s_BC[:].bitcast(F32)[:, 0:H]
            return s_xtm[:, (i % 2) * 2048 + 1024:
                          (i % 2) * 2048 + 1024 + rowlen]

        @block.sync
        def _(sy):
            for i, (src, dst, j, rowlen, mode) in enumerate(wjobs):
                if rowlen == H:
                    wt(sy, "te_prep1")
                    if i >= 1:
                        wt(sy, f"wp_read_{i-1}")
                if i >= 2:
                    wt(sy, f"wp_read_{i-2}")
                tick(f"wp{i % 2}", sy.dma_start(
                    out=wp_in(i, rowlen),
                    in_=src[j * 128:(j + 1) * 128, :]), 16)
                mark(f"wp_dma_{i}", f"wp{i % 2}")

        @block.scalar
        def _(sc):
            for i, (src, dst, j, rowlen, mode) in enumerate(wjobs):
                par = i % 2
                wt(sc, f"wp_dma_{i}")
                src_sb = wp_in(i, rowlen)
                if mode == "q64":
                    if i >= 1:
                        wt(sc, f"wp_clip_{i-1}")
                    if i >= 2:
                        wt(sc, f"wp_clip_{i-2}")
                    tick("sc", sc.activation(
                        wp_t1(i, rowlen), src_sb,
                        AF.Copy, bias=RCONST, scale=64.0))
                else:
                    if i >= 2:
                        wt(sc, f"wp_te_{i-2}")
                    tick("sc", sc.activation(
                        s_hq[:, par * 4096:par * 4096 + rowlen], src_sb,
                        AF.Copy))
                mark(f"wp_read_{i}", "sc")



        @block.tensor
        def _(te):
            gidx = 0
            for i, (src, dst, j, rowlen, mode) in enumerate(wjobs):
                par = i % 2
                if mode == "q64":
                    wt(te, f"wp_clip_{i}")
                else:
                    wt(te, f"wp_read_{i}")
                if i >= 2:
                    wt(te, f"wp_stage_{i-2}")
                int_sb = s_hq[:, par * 4096:par * 4096 + rowlen]
                nk = rowlen // 128
                for k0 in range(0, nk, 4):
                    pk = pTA if gidx % 2 == 0 else pTB
                    pk16 = pk[:].bitcast(BF16)
                    if gidx < 2:
                        wt(te, "ve_prep1")
                    if gidx >= 2:
                        wt(te, f"wp_ev_{gidx-2}")
                    kn = min(4, nk - k0)
                    for kk in range(kn):
                        _last = te.transpose(pk16[:, kk * 128:(kk + 1) * 128],
                                     int_sb[:, (k0 + kk) * 128:
                                            (k0 + kk + 1) * 128],
                                     s_id16[:])
                    tick("te", _last)
                    mark(f"wp_tp_{gidx}", "te")
                    gidx += 1
                mark(f"wp_te_{i}", "te")

        item_gidx = []
        _g = 0
        for (_s, _d, _j, _rl, _m) in wjobs:
            ng = (_rl // 128 + 3) // 4
            item_gidx.append((_g, ng))
            _g += ng

        @block.vector
        def _(ve):
            def emit_clip(i):
                srcj, dstj, jj, rowlen, mode = wjobs[i]
                par = i % 2
                if mode != "q64":
                    mark(f"wp_clip_{i}", "ve")
                    return
                wt(ve, f"wp_read_{i}")
                if rowlen == H and i == HT:
                    wt(ve, "dcv_prep")
                if i >= 2:
                    wt(ve, f"wp_te_{i-2}")
                t1 = wp_t1(i, rowlen)
                ve.tensor_scalar(t1, t1, RCONST + 127.0, RCONST - 128.0,
                                 ALU.min, ALU.max)
                tick("ve", ve.tensor_scalar(
                    s_hq[:, par * 4096:par * 4096 + rowlen], t1,
                    RCONST, None, ALU.subtract))
                mark(f"wp_clip_{i}", "ve")

            def emit_evac(i):
                srcj, dstj, jj, rowlen, mode = wjobs[i]
                par = i % 2
                nk = rowlen // 128
                stg = s_w2s if rowlen == H else s_w1s
                stg_off = par * rowlen
                if i >= 2:
                    wt(ve, f"wp_scr_{i-2}")
                g0, ng = item_gidx[i]
                for gg in range(ng):
                    gidx = g0 + gg
                    k0 = gg * 4
                    wt(ve, f"wp_tp_{gidx}")
                    pk16 = (pTA if gidx % 2 == 0 else pTB)[:].bitcast(BF16)
                    kn = min(4, nk - k0)
                    _last = ve.tensor_copy(
                        stg[:, stg_off + k0 * 128:stg_off + (k0 + kn) * 128],
                        pk16[:, 0:kn * 128])
                    tick("ve", _last)
                    mark(f"wp_ev_{gidx}", "ve")
                mark(f"wp_stage_{i}", "ve")

            for i in range(NJ + 1):
                if i < NJ:
                    emit_clip(i)
                if i >= 1:
                    emit_evac(i - 1)

        @block.gpsimd
        def _(gp):
            wt(gp, "dconv_stage")
            tick("scr0", gp.dma_start(out=scr_dc[:, :],
                                      in_=s_BC[:, 0:G * DCONV * 128]), 16)
            mark("dcv_prep", "scr0")
            for i, (src, dst, j, rowlen, mode) in enumerate(wjobs):
                par = i % 2
                wt(gp, f"wp_stage_{i}")
                stg = s_w2s if rowlen == H else s_w1s
                tick(f"scr{i % 2}", gp.dma_start(
                    out=dst[j, :, :],
                    in_=stg[:, par * rowlen:(par + 1) * rowlen]), 16)
                mark(f"wp_scr_{i}", f"scr{i % 2}")

        for k in SEMKEYS:
            mark(f"prep_end_{k}", k)

        st3 = s_stat.rearrange("p (q k) -> p q k", k=8)
        tg_xn = [(q, h) for q in range(NSUB) for h in range(2)]
        tg_xq = [(q, h) for q in range(NSUB) for h in range(2)]

        def xqS(cc):
            sl = cc % 2
            return s_xq[:, sl * G * TC:(sl + 1) * G * TC]

        def xnS(cc):
            sl = cc % 2
            return s_xn16[:, sl * G * (3 + TC):(sl + 1) * G * (3 + TC)]

        def xn3S(cc):
            return xnS(cc).rearrange("p (g t) -> p g t", t=3 + TC)

        def x16S(q):
            return s_xq16[:, 0:D]

        def t1S(j):
            sl = j % 3
            return s_t1[:, sl * TC:(sl + 1) * TC]

        def fstgS(gi):
            return s_fstg[:, 0:512]

        MU = s_s[0:1, 0:TC]
        VCH = s_t1[0:1, 0:TC]
        MUR = s_t1[0:1, TC:2 * TC]

        def barrier(eng):
            for k in ("gp", "te", "ve", "sc", "scr0", "scr1"):
                wt(eng, f"prep_end_{k}")

        def mkpw(cc):
            def pw(eng, name):
                if cc > 0:
                    wt(eng, f"c{cc-1}_" + name)
            return pw

        def em_dum(sc, cc):
            P = f"c{cc}_"
            wt(sc, P + "x")
            for q in range(NSUB):
                sc.activation(
                    s_xqtm[:, 0:D], s_xtm[:, q * D:(q + 1) * D], AF.Copy,
                    accum_out=st3[:, q, 0:1])
                _last = sc.activation(
                    s_xqtm[:, 0:D], s_xtm[:, q * D:(q + 1) * D], AF.Square,
                    accum_out=st3[:, q, 1:2])
            tick("sc", _last)
            mark(P + "ln1_sum", "sc")
            mark(P + "ln1_sq", "sc")

        def em_ln1a(ve, cc):
            P = f"c{cc}_"

            def vst(inst):
                tick("ve", inst)
                ve.wait_ge(sems["ve"], L.cnt["ve"])

            wt(ve, P + "ln1_sq")
            ve.tensor_scalar(st3[:, :, 2:3], st3[:, :, 0:1], 1.0 / D,
                             None, ALU.mult)
            vst(ve.tensor_scalar(st3[:, :, 3:4], st3[:, :, 1:2], 1.0 / D,
                                 None, ALU.mult))
            vst(ve.tensor_tensor(st3[:, :, 4:5], st3[:, :, 2:3],
                                 st3[:, :, 2:3], ALU.mult))
            vst(ve.tensor_tensor(st3[:, :, 5:6], st3[:, :, 3:4],
                                 st3[:, :, 4:5], ALU.subtract))
            tick("ve", ve.tensor_scalar(st3[:, :, 5:6], st3[:, :, 5:6],
                                        LN_EPS, None, ALU.add))
            mark(P + "ln1_var", "ve")

        def em_ln1sd(sc, cc):
            P = f"c{cc}_"
            wt(sc, P + "ln1_var")
            tick("sc", sc.activation(st3[:, :, 5:6], st3[:, :, 5:6],
                                     AF.Sqrt))
            mark(P + "ln1_sd", "sc")

        def em_ln1b(ve, cc):
            P = f"c{cc}_"

            def vst(inst):
                tick("ve", inst)
                ve.wait_ge(sems["ve"], L.cnt["ve"])

            wt(ve, P + "ln1_sd")
            vst(ve.reciprocal(st3[:, :, 6:7], st3[:, :, 5:6]))
            vst(ve.tensor_tensor(st3[:, :, 7:8], st3[:, :, 2:3],
                                 st3[:, :, 6:7], ALU.mult))
            vst(ve.tensor_scalar(st3[:, :, 7:8], st3[:, :, 7:8],
                                 -1.0, None, ALU.mult))
            for q in range(NSUB):
                tick("ve", ve.tensor_scalar(
                    s_xtm[:, q * D:(q + 1) * D],
                    s_xtm[:, q * D:(q + 1) * D],
                    st3[:, q, 6:7], st3[:, q, 7:8],
                    ALU.mult, ALU.add))
                mark(P + f"ln1_ap_{q}", "ve")
            mark(P + "ln1_ap", "ve")

        def em_xq1(sc, cc, q):
            P = f"c{cc}_"
            wt(sc, P + f"ln1_ap_{q}")
            if q >= 1:
                wt(sc, P + f"xqclip_{q-1}")
            tick("sc", sc.activation(
                s_xqtm[:, 0:D], s_xtm[:, q * D:(q + 1) * D], AF.Copy,
                bias=RCONST, scale=16.0))
            mark(P + f"xq1_{q}", "sc")

        def em_clip(ve, cc, q):
            P = f"c{cc}_"
            wt(ve, P + f"xq1_{q}")
            if q >= 1:
                wt(ve, P + f"tp_xq_{q-1}_1")
            ve.tensor_scalar(s_xqtm[:, 0:D], s_xqtm[:, 0:D],
                             RCONST + 127.0, RCONST - 128.0,
                             ALU.min, ALU.max)
            tick("ve", ve.tensor_scalar(
                x16S(q)[:, 0:D], s_xqtm[:, 0:D], RCONST, None,
                ALU.subtract))
            mark(P + f"xqclip_{q}", "ve")

        def em_fet(te, cc, gis):
            P = f"c{cc}_"
            for gi in gis:
                pk = pBCa if gi % 2 == 0 else pBCb
                if gi >= 2:
                    wt(te, P + f"tpev_{gi-2}")
                elif cc >= 2:
                    wt(te, f"c{cc-2}_ffn_fadd_{6 + gi}")
                if gi < 8:
                    q, half = tg_xn[gi]
                    wt(te, P + f"ln1_ap_{q}")
                    for gg in range(4):
                        g = half * 4 + gg
                        _last = te.transpose(
                            pk[:, gg * 128:(gg + 1) * 128],
                            s_xtm[:, q * D + g * 128:q * D + (g + 1) * 128],
                            s_id32[:])
                    tick("te", _last)
                    mark(P + f"tp_xn_{q}_{half}", "te")
                else:
                    q, half = tg_xq[gi - 8]
                    wt(te, P + f"xqclip_{q}")
                    pk16 = pk[:].bitcast(BF16)
                    for gg in range(4):
                        g = half * 4 + gg
                        _last = te.transpose(
                            pk16[:, gg * 128:(gg + 1) * 128],
                            x16S(q)[:, g * 128:(g + 1) * 128],
                            s_id16[:])
                    tick("te", _last)
                    mark(P + f"tp_xq_{q}_{half}", "te")

        def em_fev(sc, cc, gis):
            P = f"c{cc}_"
            xn3 = xn3S(cc)
            xq3 = xqS(cc).rearrange("p (g t) -> p g t", t=TC)
            for gi in gis:
                pk = pBCa if gi % 2 == 0 else pBCb
                if gi < 8:
                    q, half = tg_xn[gi]
                    wt(sc, P + f"tp_xn_{q}_{half}")
                    dst = xn3[:, half * 4:(half + 1) * 4,
                              3 + q * 128:3 + (q + 1) * 128]
                    _last = sc.copy(
                        dst, pk[:].rearrange("p (g t) -> p g t", t=128))
                else:
                    q, half = tg_xq[gi - 8]
                    wt(sc, P + f"tp_xq_{q}_{half}")
                    dst = xq3[:, half * 4:(half + 1) * 4,
                              q * 128:(q + 1) * 128]
                    pk16 = pk[:].bitcast(BF16)
                    _last = sc.copy(
                        dst, pk16[:, 0:512].rearrange(
                            "p (g t) -> p g t", t=128))
                tick("sc", _last)
                mark(P + f"tpev_{gi}", "sc")
                if gi == 7:
                    mark(P + "xn_ready", "sc")
                if gi == 15:
                    mark(P + "xq_ready", "sc")

        def em_conv(te, cc, gs):
            P = f"c{cc}_"
            xn3 = xn3S(cc)
            for g in gs:
                pk = pBCa if g % 2 == 0 else pBCb
                if g == 0:
                    wt(te, P + "tpev_14")
                    wt(te, P + "xn_ready")
                    if cc > 0:
                        wt(te, f"c{cc-1}_halo")
                elif g == 1:
                    wt(te, P + "tpev_15")
                else:
                    wt(te, P + f"silu_{g-2}")
                if g == 0 and cc != 1:
                    wt(te, P + "dcv")
                for k in range(DCONV):
                    kk = DCONV - 1 - k
                    _last = te.matmul(
                        pk[:],
                        s_w2s[:, (g * DCONV + kk) * 128:
                              (g * DCONV + kk + 1) * 128],
                        xn3[:, g, 3 - k:3 - k + TC],
                        start=(k == 0), stop=(k == DCONV - 1))
                tick("te", _last)
                mark(P + f"cv_{g}", "te")

        def em_silu(sc, cc, gs):
            P = f"c{cc}_"
            for g in gs:
                pk = pBCa if g % 2 == 0 else pBCb
                wt(sc, P + f"cv_{g}")
                if cc > 0:
                    wt(sc, f"c{cc-1}_yg_{g}")
                tick("sc", sc.activation(
                    s_u16[:, g * TC:(g + 1) * TC], pk[:], AF.Silu,
                    bias=s_cols[:, OF_CONVB + g:OF_CONVB + g + 1]))
                mark(P + f"silu_{g}", "sc")

        def em_btct(te, cc):
            P = f"c{cc}_"
            for g in range(G):
                wt(te, P + f"silu_{g}")
                if g == 0:
                    wt(te, P + "silu_6")
                te.matmul(pBCa[0:NST, :],
                          s_BpCpT[:, g * NST:(g + 1) * NST],
                          s_u16[:, g * TC:(g + 1) * TC],
                          start=(g == 0), stop=(g == G - 1))
            for g in range(G):
                _last = te.matmul(pBCb[0:NST, :],
                                  s_BpCpT[:, (G + g) * NST:
                                          (G + g + 1) * NST],
                                  s_u16[:, g * TC:(g + 1) * TC],
                                  start=(g == 0), stop=(g == G - 1))
            tick("te", _last)
            mark(P + "btct_mm", "te")

        def em_btev(ve, cc):
            P = f"c{cc}_"
            wt(ve, P + "btct_mm")
            if cc > 0:
                wt(ve, f"c{cc-1}_bcmm_31")
            ve.tensor_scalar(s_bt[0:NST, 0:TC], pBCa[0:NST, :],
                             s_bpb[:, 0:1], None, ALU.add)
            tick("ve", ve.tensor_scalar(s_bt[0:NST, TC:2 * TC],
                                        pBCb[0:NST, :],
                                        s_bpb[:, 1:2], None, ALU.add))
            mark(P + "btct_ev", "ve")

        def em_bcmm(te, cc, gis):
            P = f"c{cc}_"
            for gi in gis:
                n, which = gi // 2, gi % 2
                pk = pBCa if gi % 2 == 0 else pBCb
                if gi < 2:
                    wt(te, P + "btct_ev")
                else:
                    wt(te, P + f"bcev_{gi-2}")
                tick("te", te.matmul(
                    pk[:], s_sel[:, n * 128:(n + 1) * 128],
                    s_bt[0:NST, which * TC:(which + 1) * TC],
                    start=True, stop=True))
                mark(P + f"bcmm_{gi}", "te")

        def em_bcev(sc, cc):
            P = f"c{cc}_"
            if cc > 0:
                wt(sc, f"c{cc-1}_z_{G * NST - 1}")
            for gi in range(2 * NST):
                n, which = gi // 2, gi % 2
                pk = pBCa if gi % 2 == 0 else pBCb
                wt(sc, P + f"bcmm_{gi}")
                tick("sc", sc.copy(
                    s_BC[:, (which * NST + n) * TC:
                         (which * NST + n + 1) * TC], pk[:]))
                mark(P + f"bcev_{gi}", "sc")

        def em_gate(te, cc):
            P = f"c{cc}_"
            for o in range(G):
                wt(te, P + f"gs_{o}")
                wt(te, P + "xn_ready")
                if o == 0:
                    wt(te, f"c{cc-1}_res_6")
                elif o == 1:
                    wt(te, f"c{cc-1}_res_7")
                else:
                    wt(te, P + f"gate_ev_{o-2}")
                pk = pMA if o % 2 == 0 else pMB
                xn3 = xn3S(cc)
                for k in range(G):
                    _last = te.matmul(
                        pk[:],
                        s_gs[:, (o % 2) * 1024 + k * 128:
                             (o % 2) * 1024 + (k + 1) * 128],
                        xn3[:, k, 3:3 + TC],
                        start=(k == 0), stop=(k == G - 1))
                tick("te", _last)
                mark(P + f"gate_mm_{o}", "te")

        def em_gateev(sc, cc):
            P = f"c{cc}_"
            pw = mkpw(cc)
            pw(sc, "out_mm_7")
            for o in range(G):
                pk = pMA if o % 2 == 0 else pMB
                wt(sc, P + f"gate_mm_{o}")
                tick("sc", sc.activation(
                    s_ga[:, o * TC:(o + 1) * TC], pk[:], AF.Silu,
                    bias=s_cols[:, OF_GATEB + o:OF_GATEB + o + 1]))
                mark(P + f"gate_ev_{o}", "sc")

        def em_f1(te, cc, j):
            P = f"c{cc}_"
            wt(te, P + f"w1_{j}")
            if j == 0:
                wt(te, P + "xq_ready")
                if cc > 0:
                    wt(te, f"c{cc-1}_ffn_h1_30")
            elif j == 1:
                if cc > 0:
                    wt(te, f"c{cc-1}_ffn_h1_31")
            else:
                wt(te, P + f"ffn_h1_{j-2}")
            bk = pTA if j % 2 == 0 else pTB
            xq = xqS(cc)
            for k in range(G):
                _last = te.matmul(bk[:],
                                  s_w1s[:, (j % 2) * 1024 + k * 128:
                                        (j % 2) * 1024 + (k + 1) * 128],
                                  xq[:, k * TC:(k + 1) * TC],
                                  start=(k == 0), stop=(k == G - 1))
            tick("te", _last)
            mark(P + f"ffn_mm1_{j}", "te")

        def em_h1(sc, cc, j):
            P = f"c{cc}_"
            if j == 0 and cc > 0:
                wt(sc, f"c{cc-1}_ln2_bc")
            wt(sc, P + f"ffn_mm1_{j}")
            bk = pTA if j % 2 == 0 else pTB
            tick("sc", sc.activation(t1S(j), bk[:], AF.Relu,
                                     bias=s_b1qC[:, j:j + 1],
                                     scale=1.0 / 64.0))
            mark(P + f"ffn_h1_{j}", "sc")

        def em_h3(sc, cc, j):
            P = f"c{cc}_"
            if j == 0 and cc > 0:
                wt(sc, f"c{cc-1}_ffn_mm2_{G-1}")
            wt(sc, P + f"gpclip_{j}")
            tick("sc", sc.activation(
                s_hq[:, j * TC:(j + 1) * TC], t1S(j), AF.Copy,
                bias=-RCONST))
            mark(P + f"ffn_h3_{j}", "sc")

        def em_faddev(sc, cc, os_):
            P = f"c{cc}_"
            for o in os_:
                pk = (pMA, pMB, pBCa, pBCb)[o % 4]
                wt(sc, P + f"ffn_mm2_{o}")
                tick("sc", sc.activation(
                    s_zres[:, o * TC:(o + 1) * TC], pk[:], AF.Copy,
                    scale=1.0 / 1024.0))
                mark(P + f"ffn_fadd_{o}", "sc")

        def mk_f2jobs(te, cc):
            P = f"c{cc}_"
            jobs = []
            for o in range(G):
                pk = (pMA, pMB, pBCa, pBCb)[o % 4]

                def mk_start(o=o, pk=pk):
                    def f():
                        wt(te, P + f"w2_{o}")
                        if o == 0:
                            if cc + 1 < NCH:
                                wt(te, f"c{cc+1}_gate_ev_6")
                            else:
                                wt(te, P + "res_6")
                            wt(te, P + f"ffn_h3_{HT-1}")
                        elif o == 1:
                            if cc + 1 < NCH:
                                wt(te, f"c{cc+1}_gate_ev_7")
                            else:
                                wt(te, P + "res_7")
                        elif o in (2, 3):
                            wt(te, P + "attn_ready")
                        else:
                            wt(te, P + f"ffn_fadd_{o-4}")
                        te.matmul(pk[:], s_w2s[:, (o % 2) * 4096:
                                                (o % 2) * 4096 + 128],
                                  s_hq[:, 0:TC], start=True, stop=False)
                    return f

                jobs.append(mk_start())
                for k in range(1, HT):
                    def mk_mid(o=o, k=k, pk=pk):
                        def f():
                            te.matmul(
                                pk[:],
                                s_w2s[:, (o % 2) * 4096 + k * 128:
                                      (o % 2) * 4096 + (k + 1) * 128],
                                s_hq[:, k * TC:(k + 1) * TC],
                                start=False, stop=False)
                        return f
                    jobs.append(mk_mid())

                def mk_seed(o=o, pk=pk):
                    def f():
                        wt(te, P + "attn_ready")
                        tick("te", te.matmul(
                            pk[:], s_id32[:],
                            s_zres[:, o * TC:(o + 1) * TC],
                            start=False, stop=True))
                        mark(P + f"ffn_mm2_{o}", "te")
                    return f
                jobs.append(mk_seed())
            return jobs

        def em_fint(te, cc, gis):
            P = f"c{cc}_"
            for gi in gis:
                q, half = gi // 2, gi % 2
                pk = pMA if gi % 2 == 0 else pMB
                if gi == 0:
                    wt(te, P + "ffn_fadd_4")
                elif gi == 1:
                    wt(te, P + "ffn_fadd_5")
                else:
                    wt(te, P + f"fin_ev_{gi-2}")
                for gg in range(4):
                    g = half * 4 + gg
                    wt(te, P + f"ffn_fadd_{g}")
                    _last = te.transpose(
                        pk[:, gg * 128:(gg + 1) * 128],
                        s_zres[:, g * TC + q * 128:g * TC + (q + 1) * 128],
                        s_id32[:])
                tick("te", _last)
                mark(P + f"fin_tp_{gi}", "te")

        def em_finev(sc, cc, gis):
            P = f"c{cc}_"
            for gi in gis:
                pk = pMA if gi % 2 == 0 else pMB
                wt(sc, P + f"fin_tp_{gi}")
                if gi == 0:
                    if cc > 0:
                        wt(sc, f"c{cc-1}_store_7")
                else:
                    wt(sc, P + f"store_{gi-1}")
                tick("sc", sc.copy(fstgS(gi), pk[:]))
                mark(P + f"fin_ev_{gi}", "sc")

        def em_store(gp, cc, gis):
            P = f"c{cc}_"
            t0 = cc * TC
            for gi in gis:
                q, half = gi // 2, gi % 2
                wt(gp, P + f"fin_ev_{gi}")
                tick(f"st{gi % 2}", gp.dma_start(
                    out=dout[t0 + q * 128:t0 + (q + 1) * 128,
                             half * 512:(half + 1) * 512],
                    in_=fstgS(gi), accum_op=ALU.add), 16)
                mark(P + f"store_{gi}", f"st{gi % 2}")

        def em_out(te, cc, o):
            P = f"c{cc}_"
            wt(te, P + f"os_{o}")
            if o == 0:
                for g in range(G):
                    wt(te, P + f"yg_{g}")
                if cc > 0:
                    wt(te, f"c{cc-1}_fin_ev_6")
            elif o == 1:
                if cc > 0:
                    wt(te, f"c{cc-1}_fin_ev_7")
            else:
                wt(te, P + f"res_{o-2}")
            pk = pMA if o % 2 == 0 else pMB
            for k in range(G):
                _last = te.matmul(
                    pk[:],
                    s_os[:, (o % 2) * 1024 + k * 128:
                         (o % 2) * 1024 + (k + 1) * 128],
                    s_ga[:, k * TC:(k + 1) * TC],
                    start=(k == 0), stop=(k == G - 1))
            tick("te", _last)
            mark(P + f"out_mm_{o}", "te")

        def em_halo(ve, cc):
            P = f"c{cc}_"
            src3 = xn3S(cc)
            dst3 = xn3S(cc + 1)
            if (cc + 1) % (NCH // BLOC) != 0:
                tick("ve", ve.tensor_copy(dst3[:, :, 0:3],
                                          src3[:, :, TC:TC + 3]))
            else:
                tick("ve", ve.memset(dst3[:, :, 0:3], 0.0))
            mark(P + "halo", "ve")

        def em_vclip(ve, cc, js):
            P = f"c{cc}_"
            for j in js:
                wt(ve, P + f"ffn_h1_{j}")
                tick("ve", ve.tensor_scalar(
                    t1S(j), t1S(j), RCONST, RCONST + 127.0,
                    ALU.add, ALU.min))
                mark(P + f"gpclip_{j}", "ve")

        def em_yg(ve, cc, g):
            P = f"c{cc}_"
            pk = pYA if g % 2 == 0 else pYB
            wt(ve, P + f"ymm_{g * NST + NST - 1}")
            wt(ve, P + f"gate_ev_{g}")
            tick("ve", ve.tensor_tensor(
                s_ga[:, g * TC:(g + 1) * TC], pk[:],
                s_ga[:, g * TC:(g + 1) * TC], ALU.mult))
            mark(P + f"yg_{g}", "ve")

        @block.sync
        def _(sy):
            tick("pf", sy.dma_start(out=dout[:], in_=dx[:]), 16)
            mark("prefill", "pf")
            barrier(sy)
            tick("dc", sy.dma_start(out=s_w2s[:, 0:G * DCONV * 128],
                                    in_=scr_dc[:, :]), 16)
            mark("c0_dcv", "dc")
            for q in range(NSUB):
                tick("x", sy.dma_start(
                    out=s_xtm[:, q * D:(q + 1) * D],
                    in_=dx[q * 128:(q + 1) * 128, :]), 16)
            mark("c0_x", "x")
            wt(sy, "c0_ln1_sq")
            wt(sy, "c0_ln1_ap")
            wt(sy, "c0_tp_xn_3_1")
            wt(sy, "c0_xq1_3")
            for q in range(NSUB):
                tick("x", sy.dma_start(
                    out=s_xtm[:, q * D:(q + 1) * D],
                    in_=dx[TC + q * 128:TC + (q + 1) * 128, :]), 16)
            mark("c1_x", "x")
            for o in range(G):
                if o >= 2:
                    wt(sy, f"c0_gate_mm_{o-2}")
                tick(f"gs{o % 2}", sy.dma_start(
                    out=s_gs[:, (o % 2) * 1024:(o % 2 + 1) * 1024],
                    in_=scr_g[o, :, :]), 16)
                mark(f"c0_gs_{o}", f"gs{o % 2}")
            for j in range(HT):
                if j >= 2:
                    wt(sy, f"c0_ffn_mm1_{j-2}")
                tick(f"w1{j % 2}", sy.dma_start(
                    out=s_w1s[:, (j % 2) * 1024:(j % 2 + 1) * 1024],
                    in_=scr_w1[j, :, :]), 16)
                mark(f"c0_w1_{j}", f"w1{j % 2}")

        @block.scalar
        def _(sc):
            barrier(sc)
            em_dum(sc, 0)

        @block.vector
        def _(ve):
            barrier(ve)
            em_ln1a(ve, 0)

        @block.scalar
        def _(sc):
            em_ln1sd(sc, 0)

        @block.vector
        def _(ve):
            em_ln1b(ve, 0)

        @block.scalar
        def _(sc):
            em_xq1(sc, 0, 0)
            em_xq1(sc, 0, 1)
            em_fev(sc, 0, range(0, 10))
            em_xq1(sc, 0, 2)
            em_fev(sc, 0, range(10, 12))
            em_xq1(sc, 0, 3)
            em_fev(sc, 0, range(12, 16))

        @block.vector
        def _(ve):
            for q in range(NSUB):
                em_clip(ve, 0, q)

        @block.tensor
        def _(te):
            barrier(te)
            em_fet(te, 0, range(16))

        @block.tensor
        def _(te):
            em_conv(te, 0, range(G))

        @block.scalar
        def _(sc):
            em_silu(sc, 0, range(G))

        @block.tensor
        def _(te):
            em_btct(te, 0)

        @block.vector
        def _(ve):
            em_btev(ve, 0)

        @block.tensor
        def _(te):
            em_bcmm(te, 0, range(2 * NST))

        @block.scalar
        def _(sc):
            em_bcev(sc, 0)

        @block.tensor
        def _(te):
            P = "c0_"
            for o in range(G):
                wt(te, P + f"gs_{o}")
                wt(te, P + "xn_ready")
                if o >= 2:
                    wt(te, P + f"gate_ev_{o-2}")
                pk = pMA if o % 2 == 0 else pMB
                xn3 = xn3S(0)
                for k in range(G):
                    _last = te.matmul(
                        pk[:],
                        s_gs[:, (o % 2) * 1024 + k * 128:
                             (o % 2) * 1024 + (k + 1) * 128],
                        xn3[:, k, 3:3 + TC],
                        start=(k == 0), stop=(k == G - 1))
                tick("te", _last)
                mark(P + f"gate_mm_{o}", "te")

        for c in range(NCH):
            P = f"c{c}_"
            P1 = f"c{c-1}_"
            nxt = c + 1 < NCH
            seq_start = (c % (NCH // BLOC) == 0)

            @block.sync
            def _(sy, c=c, P=P, nxt=nxt):
                pw = mkpw(c)
                def em_os(o):
                    if o >= 2:
                        wt(sy, P + f"out_mm_{o-2}")
                    else:
                        pw(sy, f"out_mm_{o + 6}")
                    tick(f"os{o % 2}", sy.dma_start(
                        out=s_os[:, (o % 2) * 1024:(o % 2 + 1) * 1024],
                        in_=scr_o[o, :, :]), 16)
                    mark(P + f"os_{o}", f"os{o % 2}")

                def em_w1(j):
                    Pn = f"c{c+1}_"
                    if j >= 2:
                        wt(sy, Pn + f"ffn_mm1_{j-2}")
                    else:
                        wt(sy, P + f"ffn_mm1_{j + 30}")
                    tick(f"w1{j % 2}", sy.dma_start(
                        out=s_w1s[:, (j % 2) * 1024:(j % 2 + 1) * 1024],
                        in_=scr_w1[j, :, :]), 16)
                    mark(Pn + f"w1_{j}", f"w1{j % 2}")

                def em_w2(o):
                    if o == 0 and nxt:
                        wt(sy, f"c{c+1}_cv_7")
                    if o >= 2:
                        wt(sy, P + f"ffn_mm2_{o-2}")
                    else:
                        pw(sy, f"ffn_mm2_{o + 6}")
                    tick(f"w2{o % 2}", sy.dma_start(
                        out=s_w2s[:, (o % 2) * 4096:(o % 2 + 1) * 4096],
                        in_=scr_w2[o, :, :]), 16)
                    mark(P + f"w2_{o}", f"w2{o % 2}")

                for o in range(G):
                    em_os(o)
                if nxt:
                    Pn = f"c{c+1}_"
                    for o in range(G):
                        if o >= 2:
                            wt(sy, Pn + f"gate_mm_{o-2}")
                        else:
                            wt(sy, P + f"gate_mm_{o + 6}")
                        tick(f"gs{o % 2}", sy.dma_start(
                            out=s_gs[:, (o % 2) * 1024:(o % 2 + 1) * 1024],
                            in_=scr_g[o, :, :]), 16)
                        mark(Pn + f"gs_{o}", f"gs{o % 2}")
                    em_w1(0)
                    em_w1(1)
                    for o in range(G):
                        em_w2(o)
                    for j in range(2, HT):
                        em_w1(j)
                else:
                    for o in range(G):
                        em_w2(o)

            @block.tensor
            def _(te, c=c, P=P, P1=P1, nxt=nxt):
                f2jobs = mk_f2jobs(te, c - 1) if c >= 1 else []
                ji = 0
                for g in range(G):
                    pk = pYA if g % 2 == 0 else pYB
                    if g < 2:
                        if c > 0:
                            wt(te, P1 + f"yg_{6 + g}")
                    else:
                        wt(te, P + f"yg_{g-2}")
                    te.matmul(pk[:], s_diagD[:, g * 128:(g + 1) * 128],
                              s_u16[:, g * TC:(g + 1) * TC],
                              start=True, stop=False)
                    zslots = (s_z[:, 0:TC], s_z16[:, 0:TC],
                              s_z16[:, TC:2 * TC])
                    for n in range(NST):
                        idx = g * NST + n
                        wt(te, P + f"z_{idx}")
                        tick("te", te.matmul(
                            pk[:], s_id16[:], zslots[idx % 3],
                            start=False, stop=(n == NST - 1)))
                        mark(P + f"ymm_{idx}", "te")
                        cap = 4 * (HT + 1) if g == 0 else len(f2jobs)
                        for _ in range(9):
                            if ji < min(cap, len(f2jobs)):
                                f2jobs[ji]()
                                ji += 1
                        if n == 7 and g >= 1:
                            em_f1(te, c, 4 * (g - 1) + 2)
                            em_f1(te, c, 4 * (g - 1) + 3)
                    em_f1(te, c, 4 * g)
                    em_f1(te, c, 4 * g + 1)
                    if nxt:
                        if g == 2:
                            em_fet(te, c + 1, range(0, 4))
                        elif g == 3:
                            em_fet(te, c + 1, range(4, 10))
                        elif g == 4:
                            em_fet(te, c + 1, range(10, 12))
                        elif g == 5:
                            em_fet(te, c + 1, range(12, 14))
                        elif g == 6:
                            em_fet(te, c + 1, range(14, 16))
                            em_conv(te, c + 1, range(0, 2))
                        elif g == 7:
                            em_conv(te, c + 1, range(2, 6))
                    if c >= 1:
                        if g == 3:
                            em_fint(te, c - 1, range(0, 4))
                        elif g == 4:
                            em_fint(te, c - 1, range(4, 8))
                assert ji == len(f2jobs)
                em_f1(te, c, 4 * (G - 1) + 2)
                em_f1(te, c, 4 * (G - 1) + 3)
                if nxt:
                    em_conv(te, c + 1, range(6, 8))
                    em_btct(te, c + 1)
                for o in range(G):
                    em_out(te, c, o)
                    if nxt:
                        em_bcmm(te, c + 1, range(4 * o, 4 * o + 4))
                bP = f"c{c+1}_" if nxt else P
                for g in range(G):
                    par = g % 2
                    wt(te, P + f"ln2cast_{g}")
                    if g == 0:
                        wt(te, bP + "bcev_30")
                    tick("te", te.matmul(
                        pBCa[0:1, :], s_ones16[:],
                        s_z16[:, par * TC:(par + 1) * TC],
                        start=(g == 0), stop=(g == G - 1)))
                    mark(P + f"ln2mu_{g}", "te")
                    wt(te, P + f"ln2sqc_{g}")
                    if g == 0:
                        wt(te, bP + "bcev_31")
                    tick("te", te.matmul(
                        pBCb[0:1, :], s_ones16[:],
                        s_z16[:, par * TC:(par + 1) * TC],
                        start=(g == 0), stop=(g == G - 1)))
                    mark(P + f"ln2sq_{g}", "te")
                wt(te, P + "ln2_r")
                te.matmul(pBCa[:], s_ones1[:], VCH, start=True, stop=True)
                tick("te", te.matmul(pBCb[:], s_ones1[:], MUR,
                                     start=True, stop=True))
                mark(P + "ln2_bc", "te")
                if nxt:
                    em_gate(te, c + 1)

            @block.scalar
            def _(sc, c=c, P=P, P1=P1, nxt=nxt):
                if nxt:
                    em_dum(sc, c + 1)
                em_gateev(sc, c)

                def inj(j):
                    if j == 1:
                        if nxt:
                            em_ln1sd(sc, c + 1)
                        if c >= 1:
                            em_faddev(sc, c - 1, (0, 1, 2, 3))
                    elif j == 3:
                        if nxt:
                            em_xq1(sc, c + 1, 0)
                        if c >= 1:
                            em_faddev(sc, c - 1, (4, 5))
                    elif j == 5:
                        if c >= 1:
                            em_faddev(sc, c - 1, (6, 7))
                    elif j == 7:
                        if nxt:
                            em_fev(sc, c + 1, range(0, 4))
                    elif j == 9:
                        if nxt:
                            em_xq1(sc, c + 1, 1)
                    elif j == 11:
                        if nxt:
                            em_fev(sc, c + 1, range(4, 8))
                        if c >= 1:
                            em_finev(sc, c - 1, range(0, 2))
                    elif j == 13:
                        if nxt:
                            em_fev(sc, c + 1, range(8, 10))
                            em_xq1(sc, c + 1, 2)
                        if c >= 1:
                            em_finev(sc, c - 1, range(2, 4))
                    elif j == 15:
                        if c >= 1:
                            em_finev(sc, c - 1, range(4, 8))
                    elif j == 17:
                        if nxt:
                            em_fev(sc, c + 1, range(10, 12))
                            em_xq1(sc, c + 1, 3)
                    elif j == 21:
                        if nxt:
                            em_fev(sc, c + 1, range(12, 14))
                    elif j == 25:
                        if nxt:
                            em_fev(sc, c + 1, range(14, 16))
                            em_silu(sc, c + 1, (0, 1))
                    elif j == 29:
                        if nxt:
                            em_silu(sc, c + 1, (2, 3, 4, 5, 6, 7))

                for j in range(HT):
                    if j >= 3:
                        em_h3(sc, c, j - 3)
                    em_h1(sc, c, j)
                    inj(j)
                for j in range(HT - 3, HT):
                    em_h3(sc, c, j)
                if nxt:
                    em_bcev(sc, c + 1)
                for g in range(G):
                    par = g % 2
                    wt(sc, P + f"res_{g}")
                    if g >= 2:
                        wt(sc, P + f"ln2sq_{g-2}")
                    tick("sc", sc.copy(s_z16[:, par * TC:(par + 1) * TC],
                                       s_zres[:, g * TC:(g + 1) * TC]))
                    mark(P + f"ln2cast_{g}", "sc")
                    wt(sc, P + f"ln2mu_{g}")
                    tick("sc", sc.activation(
                        s_z16[:, par * TC:(par + 1) * TC],
                        s_z16[:, par * TC:(par + 1) * TC], AF.Square))
                    mark(P + f"ln2sqc_{g}", "sc")
                wt(sc, P + "ln2_var")
                tick("sc", sc.activation(VCH, VCH, AF.Sqrt))
                mark(P + "ln2_sd", "sc")

            @block.vector
            def _(ve, c=c, P=P, P1=P1, nxt=nxt, seq_start=seq_start):
                if nxt:
                    em_halo(ve, c)
                zslots = (s_z[:, 0:TC], s_z16[:, 0:TC], s_z16[:, TC:2 * TC])
                for g in range(G):
                    for n in range(NST):
                        idx = g * NST + n
                        half = n % 2
                        zsl = zslots[idx % 3]
                        if idx >= 3:
                            wt(ve, P + f"ymm_{idx-3}")
                        if g == 0:
                            wt(ve, P + f"bcev_{2 * n}")
                            wt(ve, P + f"bcev_{2 * n + 1}")
                        ve.tensor_tensor(s_w[:],
                                         s_u16[:, g * TC:(g + 1) * TC],
                                         s_BC[:, n * TC:(n + 1) * TC],
                                         ALU.mult)
                        ci = g * NST + n
                        init = (0.0 if seq_start
                                else s_carry[:, ci:ci + 1])
                        ve.tensor_tensor_scan(
                            s_s[:, half * TC:(half + 1) * TC],
                            s_acol[:, ci:ci + 1].broadcast_to([128, TC]),
                            s_w[:], init, ALU.mult, ALU.add)
                        tick("ve", ve.tensor_tensor(
                            zsl,
                            s_s[:, half * TC:(half + 1) * TC],
                            s_BC[:, (NST + n) * TC:(NST + n + 1) * TC],
                            ALU.mult))
                        mark(P + f"z_{idx}", "ve")
                        ve.tensor_copy(
                            s_carry[:, ci:ci + 1],
                            s_s[:, (half + 1) * TC - 1:(half + 1) * TC])
                    if g >= 1:
                        em_yg(ve, c, g - 1)
                        em_vclip(ve, c, range(4 * (g - 1), 4 * g))
                    if nxt:
                        if g == 0:
                            em_ln1a(ve, c + 1)
                        elif g == 1:
                            em_ln1b(ve, c + 1)
                        elif 2 <= g <= 5:
                            em_clip(ve, c + 1, g - 2)
                em_yg(ve, c, G - 1)
                if nxt:
                    em_btev(ve, c + 1)
                if c >= 1:
                    wt(ve, P1 + "fin_tp_7")
                xn3 = xn3S(c)
                for o in range(G):
                    pk = pMA if o % 2 == 0 else pMB
                    wt(ve, P + f"out_mm_{o}")
                    tick("ve", ve.scalar_tensor_tensor(
                        s_zres[:, o * TC:(o + 1) * TC], pk[:],
                        s_cols[:, OF_OUTB + o:OF_OUTB + o + 1],
                        xn3[:, o, 3:3 + TC],
                        ALU.add, ALU.add))
                    mark(P + f"res_{o}", "ve")
                    if o == 0:
                        em_vclip(ve, c, range(4 * (G - 1), 4 * G))

                def vst(inst):
                    tick("ve", inst)
                    ve.wait_ge(sems["ve"], L.cnt["ve"])

                wt(ve, P + f"ln2sq_{G-1}")
                wt(ve, P + "ffn_h3_30")
                wt(ve, P + "ffn_h3_31")
                vst(ve.tensor_scalar(MU, pBCa[0:1, :], 1.0 / D, None,
                                     ALU.mult))
                vst(ve.tensor_scalar(VCH, pBCb[0:1, :], 1.0 / D, None,
                                     ALU.mult))
                vst(ve.tensor_tensor(MUR, MU, MU, ALU.mult))
                vst(ve.tensor_tensor(VCH, VCH, MUR, ALU.subtract))
                tick("ve", ve.tensor_scalar(VCH, VCH, LN_EPS, None,
                                            ALU.add))
                mark(P + "ln2_var", "ve")
                wt(ve, P + "ln2_sd")
                vst(ve.reciprocal(VCH, VCH))
                tick("ve", ve.tensor_tensor(MUR, MU, VCH, ALU.mult))
                mark(P + "ln2_r", "ve")
                wt(ve, P + "ln2_bc")
                z3 = s_zres.rearrange("p (g t) -> p g t", t=TC)
                ve.tensor_tensor(z3[:, :, :], z3[:, :, :],
                                 pBCa[:, None, :].broadcast_to(
                                     [128, G, TC]), ALU.mult)
                ve.tensor_tensor(z3[:, :, :], z3[:, :, :],
                                 pBCb[:, None, :].broadcast_to(
                                     [128, G, TC]), ALU.subtract)
                for g in range(G):
                    _last = ve.tensor_scalar(
                        s_zres[:, g * TC:(g + 1) * TC],
                        s_zres[:, g * TC:(g + 1) * TC],
                        s_cols[:, OF_LNG + g:OF_LNG + g + 1],
                        s_cols[:, OF_LNB2 + g:OF_LNB2 + g + 1],
                        ALU.mult, ALU.add)
                tick("ve", _last)
                mark(P + "attn_ready", "ve")

            @block.gpsimd
            def _(gp, c=c, P=P):
                wt(gp, "prefill")
                if c >= 1 and c + 1 < NCH:
                    wt(gp, f"c{c-1}_ffn_mm2_6")
                    tick("dcg", gp.dma_start(
                        out=s_w2s[:, 0:G * DCONV * 128],
                        in_=scr_dc[:, :]), 16)
                    mark(f"c{c+1}_dcv", "dcg")
                if c >= 1:
                    em_store(gp, c - 1, range(0, 8))
                if c + 2 < NCH:
                    Pn = f"c{c+1}_"
                    wt(gp, Pn + "ln1_sq")
                    wt(gp, Pn + "ln1_ap")
                    wt(gp, Pn + "tp_xn_3_1")
                    wt(gp, Pn + "xq1_3")
                    t0n = (c + 2) * TC
                    for q in range(NSUB):
                        tick("xg", gp.dma_start(
                            out=s_xtm[:, q * D:(q + 1) * D],
                            in_=dx[t0n + q * 128:
                                   t0n + (q + 1) * 128, :]), 16)
                    mark(f"c{c+2}_x", "xg")

            for k in ("gp", "te", "ve", "sc", "st0", "st1"):
                mark(P + k, k)

        cl = NCH - 1
        Pl = f"c{cl}_"

        @block.tensor
        def _(te):
            f2jobs = mk_f2jobs(te, cl)
            for f in f2jobs:
                f()
            em_fint(te, cl, range(8))

        @block.scalar
        def _(sc):
            em_faddev(sc, cl, range(G))
            em_finev(sc, cl, range(8))

        @block.gpsimd
        def _(gp):
            em_store(gp, cl, range(8))

        if DEBUG_TAPS:
            @block.gpsimd
            def _(gp):
                for k in ("gp", "te", "ve", "sc"):
                    wt(gp, f"c{NCH-1}_" + k)
                pairs = [("xn16", s_xn16), ("u16d", s_u16), ("btd", s_bt),
                         ("BCd", s_BC), ("xqd", s_xq), ("hqd", s_hq),
                         ("gad", s_ga), ("zresd", s_zres),
                         ("carryd", s_carry), ("acold", s_acol),
                         ("t1d", s_t1), ("statd", s_stat),
                         ("xtmd", s_xtm)]
                for nm, t in pairs:
                    tick("st0", gp.dma_start(out=dbg[nm], in_=t[:]), 16)
                tick("st0", gp.dma_start(out=dbg["scrw1d"],
                                        in_=scr_w1[5, :, :]), 16)
                tick("st0", gp.dma_start(out=dbg["scrw2d"],
                                        in_=scr_w2[3, :, :]), 16)

        @block.gpsimd
        def _(gp):
            if L.prev is not None:
                for k in ("st0", "st1", "st2", "st3"):
                    gp.wait_ge(sems[k], L.cnt[k])

    return nc


def build_program():
    led1 = Ledger(None)
    nc1 = bass.Bass("TRN2", target_bir_lowering=False,
                    detect_race_conditions=False)
    emit(nc1, led1)
    led2 = Ledger(led1.marks)
    nc2 = bass.Bass("TRN2", target_bir_lowering=False,
                    detect_race_conditions=False)
    emit(nc2, led2)
    assert led1.cnt == led2.cnt, (led1.cnt, led2.cnt)
    assert led1.marks == led2.marks
    return nc2


_NC_CACHE = None


def kernel(**inputs):
    global _NC_CACHE
    if _NC_CACHE is None:
        _NC_CACHE = build_program()
    nc = _NC_CACHE

    x = np.ascontiguousarray(inputs["L0_attn_x"], dtype=np.float32)
    base = {
        "conv_w": np.ascontiguousarray(
            inputs["L0_attn_conv_w"].reshape(D, DCONV), np.float32),
        "conv_b": np.ascontiguousarray(inputs["L0_attn_conv_b"], np.float32),
        "A": np.ascontiguousarray(inputs["L0_attn_A"], np.float32),
        "Bp_w": np.ascontiguousarray(inputs["L0_attn_Bp_w"], np.float32),
        "Bp_b": np.ascontiguousarray(inputs["L0_attn_Bp_b"], np.float32),
        "Cp_w": np.ascontiguousarray(inputs["L0_attn_Cp_w"], np.float32),
        "Cp_b": np.ascontiguousarray(inputs["L0_attn_Cp_b"], np.float32),
        "Dv": np.ascontiguousarray(inputs["L0_attn_D"], np.float32),
        "gate_w": np.ascontiguousarray(inputs["L0_attn_gate_w"], np.float32),
        "gate_b": np.ascontiguousarray(inputs["L0_attn_gate_b"], np.float32),
        "out_w": np.ascontiguousarray(inputs["L0_attn_out_w"], np.float32),
        "out_b": np.ascontiguousarray(inputs["L0_attn_out_b"], np.float32),
        "ln_g": np.ascontiguousarray(inputs["L0_attn_ln_g"], np.float32),
        "ln_b": np.ascontiguousarray(inputs["L0_attn_ln_b"], np.float32),
        "w1": np.ascontiguousarray(inputs["L0_ffn_w1"], np.float32),
        "b1": np.ascontiguousarray(inputs["L0_ffn_b1"], np.float32),
        "w2": np.ascontiguousarray(inputs["L0_ffn_w2"], np.float32),
        "b2": np.ascontiguousarray(inputs["L0_ffn_b2"], np.float32),
    }
    in_maps = []
    for ci in range(NCORES):
        m = dict(base)
        m["x"] = np.ascontiguousarray(
            x[ci * BLOC:(ci + 1) * BLOC].reshape(T, D))
        in_maps.append(m)

    res = run_bass_kernel_spmd(nc, in_maps, core_ids=list(range(NCORES)))
    out = np.empty((16, SEQ, D), np.float32)
    for ci in range(NCORES):
        out[ci * BLOC:(ci + 1) * BLOC] = res.results[ci]["out"].reshape(
            BLOC, SEQ, D)
    return out


# version 16
# speedup vs baseline: 1.0110x; 1.0110x over previous
import contextlib
import numpy as np

import concourse.bass as bass
import concourse.mybir as mybir
from concourse.bass_utils import run_bass_kernel_spmd

F32 = mybir.dt.float32
BF16 = mybir.dt.bfloat16
AF = mybir.ActivationFunctionType
ALU = mybir.AluOpType
AX = mybir.AxisListType

NCORES = 8
SEQ = 2048
BLOC = 2
T = SEQ * BLOC
D = 1024
H = 4096
NST = 16
DCONV = 4
LN_EPS = 1e-5
G = D // 128
HT = H // 128
TC = 512
NCH = T // TC
NSUB = TC // 128
RCONST = float(3 << 22)
DEBUG_TAPS = False


SEMKEYS = ("sy", "gp", "te", "ve", "sc", "params", "x", "xg", "pf", "dc", "dcg",
           "st0", "st1", "st2", "st3",
           "gs0", "gs1", "os0", "os1", "w10", "w11", "w20", "w21",
           "wp0", "wp1", "scr0", "scr1")


class Ledger:
    def __init__(self, prev_marks):
        self.cnt = {k: 0 for k in SEMKEYS}
        self.marks = {}
        self.prev = prev_marks

    def tick(self, sems, key, inst, k=1):
        inst.then_inc(sems[key], k)
        self.cnt[key] += k

    def mark(self, name, key):
        assert name not in self.marks, name
        self.marks[name] = (key, self.cnt[key])

    def wait(self, sems, eng, name):
        if self.prev is None:
            return
        key, val = self.prev[name]
        if val > 0:
            eng.wait_ge(sems[key], val)


def emit(nc, led):
    L = led

    dx = nc.dram_tensor("x", [T, D], F32, kind="ExternalInput").ap()
    dconv_w = nc.dram_tensor("conv_w", [D, DCONV], F32, kind="ExternalInput").ap()
    dconv_b = nc.dram_tensor("conv_b", [D], F32, kind="ExternalInput").ap()
    dA = nc.dram_tensor("A", [D, NST], F32, kind="ExternalInput").ap()
    dBpw = nc.dram_tensor("Bp_w", [NST, D], F32, kind="ExternalInput").ap()
    dBpb = nc.dram_tensor("Bp_b", [NST], F32, kind="ExternalInput").ap()
    dCpw = nc.dram_tensor("Cp_w", [NST, D], F32, kind="ExternalInput").ap()
    dCpb = nc.dram_tensor("Cp_b", [NST], F32, kind="ExternalInput").ap()
    dD = nc.dram_tensor("Dv", [D], F32, kind="ExternalInput").ap()
    dgw = nc.dram_tensor("gate_w", [D, D], F32, kind="ExternalInput").ap()
    dgb = nc.dram_tensor("gate_b", [D], F32, kind="ExternalInput").ap()
    dow = nc.dram_tensor("out_w", [D, D], F32, kind="ExternalInput").ap()
    dob = nc.dram_tensor("out_b", [D], F32, kind="ExternalInput").ap()
    dlng = nc.dram_tensor("ln_g", [D], F32, kind="ExternalInput").ap()
    dlnb = nc.dram_tensor("ln_b", [D], F32, kind="ExternalInput").ap()
    dw1 = nc.dram_tensor("w1", [H, D], F32, kind="ExternalInput").ap()
    db1 = nc.dram_tensor("b1", [H], F32, kind="ExternalInput").ap()
    dw2 = nc.dram_tensor("w2", [D, H], F32, kind="ExternalInput").ap()
    db2 = nc.dram_tensor("b2", [D], F32, kind="ExternalInput").ap()
    dout = nc.dram_tensor("out", [T, D], F32, kind="ExternalOutput").ap()
    scr_w1 = nc.dram_tensor("scr_w1", [HT, 128, 1024], BF16, kind="Internal").ap()
    scr_w2 = nc.dram_tensor("scr_w2", [G, 128, 4096], BF16, kind="Internal").ap()
    scr_g = nc.dram_tensor("scr_g", [G, 128, 1024], BF16, kind="Internal").ap()
    scr_o = nc.dram_tensor("scr_o", [G, 128, 1024], BF16, kind="Internal").ap()
    scr_dc = nc.dram_tensor("scr_dc", [128, G * DCONV * 128], BF16,
                            kind="Internal").ap()
    dbg = {}
    if DEBUG_TAPS:
        for nm, shape, dt in (
                ("xn16", [128, G * (3 + TC)], BF16),
                ("u16d", [128, G * TC], BF16),
                ("btd", [NST, 2 * TC], BF16),
                ("BCd", [128, 2 * NST * TC], BF16),
                ("xqd", [128, G * TC], BF16),
                ("hqd", [128, HT * TC], BF16),
                ("gad", [128, G * TC], BF16),
                ("zresd", [128, G * TC], F32),
                ("carryd", [128, G * NST], F32),
                ("acold", [128, G * NST], F32),
                ("t1d", [128, 4 * TC], F32),
                ("statd", [128, NSUB * 8], F32),
                ("xtmd", [128, NSUB * D], F32),
                ("scrw1d", [128, 1024], BF16),
                ("scrw2d", [128, 4096], BF16)):
            dbg[nm] = nc.dram_tensor("dbg_" + nm, shape, dt,
                                     kind="ExternalOutput").ap()

    ctx = contextlib.ExitStack()
    sb = lambda name, shape, dt=F32: ctx.enter_context(
        nc.sbuf_tensor(name, shape, dt))
    ps = lambda name: ctx.enter_context(nc.psum_tensor(name, [128, 512], F32))

    with ctx:
        s_BpCpT = sb("BpCpT", [128, 2 * G * NST], BF16)
        s_acol = sb("acol", [128, G * NST])
        s_diagD = sb("diagD", [128, G * 128], BF16)
        s_convw = sb("convw", [128, G * DCONV])
        s_cols = sb("cols", [128, 5 * G])
        OF_CONVB, OF_GATEB, OF_OUTB, OF_LNG, OF_LNB2 = (
            0, G, 2 * G, 3 * G, 4 * G)
        s_b1qC = sb("b1qC", [128, HT])
        s_sel = sb("sel", [NST, NST * 128], BF16)
        s_bpb = sb("bpb", [NST, 2])
        s_id32 = sb("id32", [128, 128])
        s_id16 = sb("id16", [128, 128], BF16)
        s_ones1 = sb("ones1", [1, 128])
        s_negC = sb("negC", [128, 1])
        s_ones16 = sb("ones16c", [128, 1], BF16)
        s_carry = sb("carry", [128, G * NST])
        s_w1s = sb("w1s", [128, 2 * 1024], BF16)
        s_w2s = sb("w2s", [128, 2 * 4096], BF16)
        s_gs = sb("gs", [128, 2 * 1024], BF16)
        s_os = sb("osb", [128, 2 * 1024], BF16)
        s_xtm = sb("xtm", [128, NSUB * D])
        s_stat = sb("stat", [128, NSUB * 8])
        s_xq16 = sb("xq16", [128, D], BF16)
        s_xqtm = sb("xqtm", [128, D])
        s_fstg = sb("fstg", [128, 512])
        s_xn16 = sb("xn16", [128, 2 * G * (3 + TC)], BF16)
        s_xq = sb("xq", [128, 2 * G * TC], BF16)
        s_u16 = sb("u16", [128, G * TC], BF16)
        s_bt = sb("bt", [NST, 2 * TC], BF16)
        s_BC = sb("BC", [128, 2 * NST * TC], BF16)
        s_w = sb("wbuf", [128, TC], BF16)
        s_s = sb("sbuf_s", [128, 2 * TC])
        s_z = sb("zbuf", [128, TC], BF16)
        s_ga = sb("ga", [128, G * TC], BF16)
        s_zres = sb("zres", [128, G * TC])
        s_z16 = sb("z16", [128, 2 * TC], BF16)
        s_hq = sb("hq", [128, HT * TC], BF16)
        s_t1 = sb("t1", [128, 4 * TC])
        pTA, pTB = ps("pTA"), ps("pTB")
        pMA, pMB = ps("pMA"), ps("pMB")
        pBCa, pBCb = ps("pBCa"), ps("pBCb")
        pYA, pYB = ps("pYA"), ps("pYB")

        sems = {k: ctx.enter_context(nc.semaphore("sem_" + k))
                for k in SEMKEYS}
        block = ctx.enter_context(nc.Block())

        def tick(key, inst, k=1):
            L.tick(sems, key, inst, k)

        def mark(name, key):
            L.mark(name, key)

        def wt(eng, name):
            L.wait(sems, eng, name)

        import contextlib as _cl

        def scoped(name):
            return nc.named_scope(name)

        @block.gpsimd
        def _(gp):
            def ss(inst):
                tick("gp", inst)
                gp.wait_ge(sems["gp"], L.cnt["gp"])

            ss(gp.memset(s_id32[:], 0.0))
            ss(nc.gpsimd.affine_select(
                out=s_id32[:], in_=s_id32[:], compare_op=ALU.not_equal,
                fill=1.0, base=0, pattern=[[-1, 128]], channel_multiplier=1))
            ss(gp.memset(s_id16[:], 0.0))
            ss(nc.gpsimd.affine_select(
                out=s_id16[:], in_=s_id16[:], compare_op=ALU.not_equal,
                fill=1.0, base=0, pattern=[[-1, 128]], channel_multiplier=1))
            ss(gp.memset(s_sel[:], 0.0))
            sel3 = s_sel.rearrange("p (n f) -> p n f", f=128)
            ss(nc.gpsimd.affine_select(
                out=sel3, in_=sel3, compare_op=ALU.not_equal,
                fill=1.0, base=0, pattern=[[-1, NST], [0, 128]],
                channel_multiplier=1))
            gp.memset(s_ones1[:], 1.0)
            gp.memset(s_negC[:], -RCONST)
            gp.memset(s_ones16[:], 1.0)
            gp.memset(s_carry[:], 0.0)
            tick("gp", gp.memset(s_xn16[:], 0.0))
            mark("gp_const", "gp")

        @block.sync
        def _(sy):
            for g in range(G):
                tick("params", sy.dma_start(
                    out=s_convw[:, g * DCONV:(g + 1) * DCONV],
                    in_=dconv_w[g * 128:(g + 1) * 128, :]), 16)
                tick("params", sy.dma_start(
                    out=s_cols[:, OF_CONVB + g:OF_CONVB + g + 1],
                    in_=dconv_b[g * 128:(g + 1) * 128][:, None]), 16)
                tick("params", sy.dma_start(
                    out=s_cols[:, OF_GATEB + g:OF_GATEB + g + 1],
                    in_=dgb[g * 128:(g + 1) * 128][:, None]), 16)
                tick("params", sy.dma_start(
                    out=s_cols[:, OF_OUTB + g:OF_OUTB + g + 1],
                    in_=dob[g * 128:(g + 1) * 128][:, None]), 16)
                tick("params", sy.dma_start(
                    out=s_cols[:, OF_LNG + g:OF_LNG + g + 1],
                    in_=dlng[g * 128:(g + 1) * 128][:, None]), 16)
                tick("params", sy.dma_start(
                    out=s_acol[:, g * NST:(g + 1) * NST],
                    in_=dA[g * 128:(g + 1) * 128, :]), 16)
                tick("params", sy.dma_start(
                    out=s_stat[:, g:g + 1],
                    in_=dD[g * 128:(g + 1) * 128][:, None]), 16)
                tick("params", sy.dma_start(
                    out=s_stat[:, 8 + g:8 + g + 1],
                    in_=dlnb[g * 128:(g + 1) * 128][:, None]), 16)
            tick("params", sy.dma_start(out=s_bpb[:, 0:1], in_=dBpb[:, None]), 16)
            tick("params", sy.dma_start(out=s_bpb[:, 1:2], in_=dCpb[:, None]), 16)
            tick("params", sy.dma_start(
                out=s_zres[0:32, 0:128],
                in_=db1.rearrange("(a b) -> a b", b=128)), 16)
            tick("params", sy.dma_start(
                out=s_zres[0:8, 200:328],
                in_=db2.rearrange("(a b) -> a b", b=128)), 16)
            tick("params", sy.dma_start(
                out=s_zres[0:NST, 400:400 + D], in_=dBpw[:, :]), 16)
            tick("params", sy.dma_start(
                out=s_zres[0:NST, 1500:1500 + D], in_=dCpw[:, :]), 16)
            mark("sy_params", "params")

        @block.scalar
        def _(sc):
            wt(sc, "sy_params")
            tick("sc", sc.activation(s_acol[:], s_acol[:], AF.Exp))
            sc.wait_ge(sems["sc"], L.cnt["sc"])
            tick("sc", sc.activation(s_acol[:], s_acol[:], AF.Exp, scale=-1.0))
            mark("sc_acol", "sc")

        @block.tensor
        def _(te):
            wt(te, "sy_params")
            wt(te, "gp_const")
            te.transpose(pTA[:, 0:32], s_zres[0:32, 0:128],
                         s_id32[0:32, 0:32])
            te.transpose(pTA[:, 32:40], s_zres[0:8, 200:328],
                         s_id32[0:8, 0:8])
            for g in range(G):
                te.transpose(pTB[:, g * 16:(g + 1) * 16],
                             s_zres[0:NST, 400 + g * 128:400 + (g + 1) * 128],
                             s_id32[0:NST, 0:NST])
            for g in range(G):
                _last = te.transpose(pTB[:, 128 + g * 16:128 + (g + 1) * 16],
                             s_zres[0:NST, 1500 + g * 128:1500 + (g + 1) * 128],
                             s_id32[0:NST, 0:NST])
            tick("te", _last)
            mark("te_prep1", "te")

        @block.vector
        def _(ve):
            def vst(inst):
                tick("ve", inst)
                ve.wait_ge(sems["ve"], L.cnt["ve"])

            wt(ve, "te_prep1")
            wt(ve, "sy_params")
            wt(ve, "gp_const")
            for g in range(G):
                ve.tensor_scalar(s_diagD[:, g * 128:(g + 1) * 128],
                                 s_id16[:], s_stat[:, g:g + 1], None, ALU.mult)
            for g in range(G):
                for k in range(DCONV):
                    i = g * DCONV + k
                    _last = ve.tensor_scalar(
                        s_BC[:, i * 128:(i + 1) * 128],
                        s_id16[:], s_convw[:, i:i + 1], None, ALU.mult)
            tick("ve", _last)
            mark("dconv_stage", "ve")
            vst(ve.tensor_scalar(s_b1qC[:], pTA[:, 0:HT], 64.0, RCONST,
                                 ALU.mult, ALU.add))
            vst(ve.tensor_scalar(s_b1qC[:], s_b1qC[:], RCONST + 127.0,
                                 RCONST - 128.0, ALU.min, ALU.max))
            vst(ve.tensor_scalar(s_b1qC[:], s_b1qC[:], RCONST, None,
                                 ALU.subtract))
            vst(ve.tensor_scalar(s_b1qC[:], s_b1qC[:], 0.25, None, ALU.mult))
            vst(ve.tensor_scalar(s_cols[:, OF_LNB2:OF_LNB2 + 8],
                                 pTA[:, 32:40], 64.0, RCONST,
                                 ALU.mult, ALU.add))
            vst(ve.tensor_scalar(s_cols[:, OF_LNB2:OF_LNB2 + 8],
                                 s_cols[:, OF_LNB2:OF_LNB2 + 8],
                                 RCONST + 127.0, RCONST - 128.0,
                                 ALU.min, ALU.max))
            vst(ve.tensor_scalar(s_cols[:, OF_LNB2:OF_LNB2 + 8],
                                 s_cols[:, OF_LNB2:OF_LNB2 + 8],
                                 RCONST, None, ALU.subtract))
            vst(ve.tensor_scalar(s_stat[:, 8:16], s_stat[:, 8:16],
                                 1024.0, None, ALU.mult))
            vst(ve.scalar_tensor_tensor(
                s_cols[:, OF_LNB2:OF_LNB2 + 8],
                s_cols[:, OF_LNB2:OF_LNB2 + 8], 16.0,
                s_stat[:, 8:16], ALU.mult, ALU.add))
            vst(ve.tensor_scalar(s_cols[:, OF_LNG:OF_LNG + 8],
                                 s_cols[:, OF_LNG:OF_LNG + 8],
                                 1024.0, None, ALU.mult))
            ve.tensor_copy(s_BpCpT[:, 0:G * NST], pTB[:, 0:G * NST])
            tick("ve", ve.tensor_copy(s_BpCpT[:, G * NST:2 * G * NST],
                                      pTB[:, 128:128 + G * NST]))
            mark("ve_prep1", "ve")

        wjobs = []
        for (src, dst, nrt, rowlen, mode) in (
                (dw1, scr_w1, HT, D, "q64"),
                (dw2, scr_w2, G, H, "q64"),
                (dgw, scr_g, G, D, "cast"),
                (dow, scr_o, G, D, "cast")):
            for j in range(nrt):
                wjobs.append((src, dst, j, rowlen, mode))
        NJ = len(wjobs)

        def wp_in(i, rowlen):
            if rowlen == H:
                return s_zres[:, 0:H]
            return s_xtm[:, (i % 2) * 2048:(i % 2) * 2048 + rowlen]

        def wp_t1(i, rowlen):
            if rowlen == H:
                return s_BC[:].bitcast(F32)[:, 0:H]
            return s_xtm[:, (i % 2) * 2048 + 1024:
                          (i % 2) * 2048 + 1024 + rowlen]

        @block.sync
        def _(sy):
            for i, (src, dst, j, rowlen, mode) in enumerate(wjobs):
                if rowlen == H:
                    wt(sy, "te_prep1")
                    if i >= 1:
                        wt(sy, f"wp_read_{i-1}")
                if i >= 2:
                    wt(sy, f"wp_read_{i-2}")
                tick(f"wp{i % 2}", sy.dma_start(
                    out=wp_in(i, rowlen),
                    in_=src[j * 128:(j + 1) * 128, :]), 16)
                mark(f"wp_dma_{i}", f"wp{i % 2}")

        @block.scalar
        def _(sc):
            for i, (src, dst, j, rowlen, mode) in enumerate(wjobs):
                par = i % 2
                wt(sc, f"wp_dma_{i}")
                src_sb = wp_in(i, rowlen)
                if mode == "q64":
                    if i >= 1:
                        wt(sc, f"wp_clip_{i-1}")
                    if i >= 2:
                        wt(sc, f"wp_clip_{i-2}")
                    tick("sc", sc.activation(
                        wp_t1(i, rowlen), src_sb,
                        AF.Copy, bias=RCONST, scale=64.0))
                else:
                    if i >= 2:
                        wt(sc, f"wp_te_{i-2}")
                    tick("sc", sc.activation(
                        s_hq[:, par * 4096:par * 4096 + rowlen], src_sb,
                        AF.Copy))
                mark(f"wp_read_{i}", "sc")



        @block.tensor
        def _(te):
            gidx = 0
            for i, (src, dst, j, rowlen, mode) in enumerate(wjobs):
                par = i % 2
                if mode == "q64":
                    wt(te, f"wp_clip_{i}")
                else:
                    wt(te, f"wp_read_{i}")
                if i >= 2:
                    wt(te, f"wp_stage_{i-2}")
                int_sb = s_hq[:, par * 4096:par * 4096 + rowlen]
                nk = rowlen // 128
                for k0 in range(0, nk, 4):
                    pk = pTA if gidx % 2 == 0 else pTB
                    pk16 = pk[:].bitcast(BF16)
                    if gidx < 2:
                        wt(te, "ve_prep1")
                    if gidx >= 2:
                        wt(te, f"wp_ev_{gidx-2}")
                    kn = min(4, nk - k0)
                    for kk in range(kn):
                        _last = te.transpose(pk16[:, kk * 128:(kk + 1) * 128],
                                     int_sb[:, (k0 + kk) * 128:
                                            (k0 + kk + 1) * 128],
                                     s_id16[:])
                    tick("te", _last)
                    mark(f"wp_tp_{gidx}", "te")
                    gidx += 1
                mark(f"wp_te_{i}", "te")

        item_gidx = []
        _g = 0
        for (_s, _d, _j, _rl, _m) in wjobs:
            ng = (_rl // 128 + 3) // 4
            item_gidx.append((_g, ng))
            _g += ng

        @block.vector
        def _(ve):
            def emit_clip(i):
                srcj, dstj, jj, rowlen, mode = wjobs[i]
                par = i % 2
                if mode != "q64":
                    mark(f"wp_clip_{i}", "ve")
                    return
                wt(ve, f"wp_read_{i}")
                if rowlen == H and i == HT:
                    wt(ve, "dcv_prep")
                if i >= 2:
                    wt(ve, f"wp_te_{i-2}")
                t1 = wp_t1(i, rowlen)
                ve.tensor_scalar(t1, t1, RCONST + 127.0, RCONST - 128.0,
                                 ALU.min, ALU.max)
                tick("ve", ve.tensor_scalar(
                    s_hq[:, par * 4096:par * 4096 + rowlen], t1,
                    RCONST, None, ALU.subtract))
                mark(f"wp_clip_{i}", "ve")

            def emit_evac(i):
                srcj, dstj, jj, rowlen, mode = wjobs[i]
                par = i % 2
                nk = rowlen // 128
                stg = s_w2s if rowlen == H else s_w1s
                stg_off = par * rowlen
                if i >= 2:
                    wt(ve, f"wp_scr_{i-2}")
                g0, ng = item_gidx[i]
                for gg in range(ng):
                    gidx = g0 + gg
                    k0 = gg * 4
                    wt(ve, f"wp_tp_{gidx}")
                    pk16 = (pTA if gidx % 2 == 0 else pTB)[:].bitcast(BF16)
                    kn = min(4, nk - k0)
                    _last = ve.tensor_copy(
                        stg[:, stg_off + k0 * 128:stg_off + (k0 + kn) * 128],
                        pk16[:, 0:kn * 128])
                    tick("ve", _last)
                    mark(f"wp_ev_{gidx}", "ve")
                mark(f"wp_stage_{i}", "ve")

            for i in range(NJ + 1):
                if i < NJ:
                    emit_clip(i)
                if i >= 1:
                    emit_evac(i - 1)

        @block.gpsimd
        def _(gp):
            wt(gp, "dconv_stage")
            tick("scr0", gp.dma_start(out=scr_dc[:, :],
                                      in_=s_BC[:, 0:G * DCONV * 128]), 16)
            mark("dcv_prep", "scr0")
            for i, (src, dst, j, rowlen, mode) in enumerate(wjobs):
                par = i % 2
                wt(gp, f"wp_stage_{i}")
                stg = s_w2s if rowlen == H else s_w1s
                tick(f"scr{i % 2}", gp.dma_start(
                    out=dst[j, :, :],
                    in_=stg[:, par * rowlen:(par + 1) * rowlen]), 16)
                mark(f"wp_scr_{i}", f"scr{i % 2}")

        for k in SEMKEYS:
            mark(f"prep_end_{k}", k)

        st3 = s_stat.rearrange("p (q k) -> p q k", k=8)
        tg_xn = [(q, h) for q in range(NSUB) for h in range(2)]
        tg_xq = [(q, h) for q in range(NSUB) for h in range(2)]

        def xqS(cc):
            sl = cc % 2
            return s_xq[:, sl * G * TC:(sl + 1) * G * TC]

        def xnS(cc):
            sl = cc % 2
            return s_xn16[:, sl * G * (3 + TC):(sl + 1) * G * (3 + TC)]

        def xn3S(cc):
            return xnS(cc).rearrange("p (g t) -> p g t", t=3 + TC)

        def x16S(q):
            return s_xq16[:, 0:D]

        def t1S(j):
            sl = j % 4
            return s_t1[:, sl * TC:(sl + 1) * TC]

        def fstgS(gi):
            return s_fstg[:, 0:512]

        MU = s_s[0:1, 0:TC]
        VCH = s_t1[0:1, 0:TC]
        MUR = s_t1[0:1, TC:2 * TC]

        def barrier(eng):
            for k in ("gp", "te", "ve", "sc", "scr0", "scr1"):
                wt(eng, f"prep_end_{k}")

        def mkpw(cc):
            def pw(eng, name):
                if cc > 0:
                    wt(eng, f"c{cc-1}_" + name)
            return pw

        def em_dum(sc, cc):
            P = f"c{cc}_"
            wt(sc, P + "x")
            for q in range(NSUB):
                sc.activation(
                    s_xqtm[:, 0:D], s_xtm[:, q * D:(q + 1) * D], AF.Copy,
                    accum_out=st3[:, q, 0:1])
                _last = sc.activation(
                    s_xqtm[:, 0:D], s_xtm[:, q * D:(q + 1) * D], AF.Square,
                    accum_out=st3[:, q, 1:2])
            tick("sc", _last)
            mark(P + "ln1_sum", "sc")
            mark(P + "ln1_sq", "sc")

        def em_ln1a(ve, cc):
            P = f"c{cc}_"

            def vst(inst):
                tick("ve", inst)
                ve.wait_ge(sems["ve"], L.cnt["ve"])

            wt(ve, P + "ln1_sq")
            ve.tensor_scalar(st3[:, :, 2:3], st3[:, :, 0:1], 1.0 / D,
                             None, ALU.mult)
            vst(ve.tensor_scalar(st3[:, :, 3:4], st3[:, :, 1:2], 1.0 / D,
                                 None, ALU.mult))
            vst(ve.tensor_tensor(st3[:, :, 4:5], st3[:, :, 2:3],
                                 st3[:, :, 2:3], ALU.mult))
            vst(ve.tensor_tensor(st3[:, :, 5:6], st3[:, :, 3:4],
                                 st3[:, :, 4:5], ALU.subtract))
            tick("ve", ve.tensor_scalar(st3[:, :, 5:6], st3[:, :, 5:6],
                                        LN_EPS, None, ALU.add))
            mark(P + "ln1_var", "ve")

        def em_ln1sd(sc, cc):
            P = f"c{cc}_"
            wt(sc, P + "ln1_var")
            tick("sc", sc.activation(st3[:, :, 5:6], st3[:, :, 5:6],
                                     AF.Sqrt))
            mark(P + "ln1_sd", "sc")

        def em_ln1b(ve, cc):
            P = f"c{cc}_"

            def vst(inst):
                tick("ve", inst)
                ve.wait_ge(sems["ve"], L.cnt["ve"])

            wt(ve, P + "ln1_sd")
            vst(ve.reciprocal(st3[:, :, 6:7], st3[:, :, 5:6]))
            vst(ve.tensor_tensor(st3[:, :, 7:8], st3[:, :, 2:3],
                                 st3[:, :, 6:7], ALU.mult))
            vst(ve.tensor_scalar(st3[:, :, 7:8], st3[:, :, 7:8],
                                 -1.0, None, ALU.mult))
            for q in range(NSUB):
                tick("ve", ve.tensor_scalar(
                    s_xtm[:, q * D:(q + 1) * D],
                    s_xtm[:, q * D:(q + 1) * D],
                    st3[:, q, 6:7], st3[:, q, 7:8],
                    ALU.mult, ALU.add))
                mark(P + f"ln1_ap_{q}", "ve")
            mark(P + "ln1_ap", "ve")

        def em_xq1(sc, cc, q):
            P = f"c{cc}_"
            wt(sc, P + f"ln1_ap_{q}")
            if q >= 1:
                wt(sc, P + f"xqclip_{q-1}")
            tick("sc", sc.activation(
                s_xqtm[:, 0:D], s_xtm[:, q * D:(q + 1) * D], AF.Copy,
                bias=RCONST, scale=16.0))
            mark(P + f"xq1_{q}", "sc")

        def em_clip(ve, cc, q):
            P = f"c{cc}_"
            wt(ve, P + f"xq1_{q}")
            if q >= 1:
                wt(ve, P + f"tp_xq_{q-1}_1")
            ve.tensor_scalar(s_xqtm[:, 0:D], s_xqtm[:, 0:D],
                             RCONST + 127.0, RCONST - 128.0,
                             ALU.min, ALU.max)
            tick("ve", ve.tensor_scalar(
                x16S(q)[:, 0:D], s_xqtm[:, 0:D], RCONST, None,
                ALU.subtract))
            mark(P + f"xqclip_{q}", "ve")

        def em_fet(te, cc, gis):
            P = f"c{cc}_"
            for gi in gis:
                pk = pBCa if gi % 2 == 0 else pBCb
                if gi >= 2:
                    wt(te, P + f"tpev_{gi-2}")
                elif cc >= 2:
                    wt(te, f"c{cc-2}_ffn_fadd_{6 + gi}")
                if gi < 8:
                    q, half = tg_xn[gi]
                    wt(te, P + f"ln1_ap_{q}")
                    for gg in range(4):
                        g = half * 4 + gg
                        _last = te.transpose(
                            pk[:, gg * 128:(gg + 1) * 128],
                            s_xtm[:, q * D + g * 128:q * D + (g + 1) * 128],
                            s_id32[:])
                    tick("te", _last)
                    mark(P + f"tp_xn_{q}_{half}", "te")
                else:
                    q, half = tg_xq[gi - 8]
                    wt(te, P + f"xqclip_{q}")
                    pk16 = pk[:].bitcast(BF16)
                    for gg in range(4):
                        g = half * 4 + gg
                        _last = te.transpose(
                            pk16[:, gg * 128:(gg + 1) * 128],
                            x16S(q)[:, g * 128:(g + 1) * 128],
                            s_id16[:])
                    tick("te", _last)
                    mark(P + f"tp_xq_{q}_{half}", "te")

        def em_fev(sc, cc, gis):
            P = f"c{cc}_"
            xn3 = xn3S(cc)
            xq3 = xqS(cc).rearrange("p (g t) -> p g t", t=TC)
            for gi in gis:
                pk = pBCa if gi % 2 == 0 else pBCb
                if gi < 8:
                    q, half = tg_xn[gi]
                    wt(sc, P + f"tp_xn_{q}_{half}")
                    dst = xn3[:, half * 4:(half + 1) * 4,
                              3 + q * 128:3 + (q + 1) * 128]
                    _last = sc.copy(
                        dst, pk[:].rearrange("p (g t) -> p g t", t=128))
                else:
                    q, half = tg_xq[gi - 8]
                    wt(sc, P + f"tp_xq_{q}_{half}")
                    dst = xq3[:, half * 4:(half + 1) * 4,
                              q * 128:(q + 1) * 128]
                    pk16 = pk[:].bitcast(BF16)
                    _last = sc.copy(
                        dst, pk16[:, 0:512].rearrange(
                            "p (g t) -> p g t", t=128))
                tick("sc", _last)
                mark(P + f"tpev_{gi}", "sc")
                if gi == 7:
                    mark(P + "xn_ready", "sc")
                if gi == 15:
                    mark(P + "xq_ready", "sc")

        def em_conv(te, cc, gs):
            P = f"c{cc}_"
            xn3 = xn3S(cc)
            for g in gs:
                pk = pBCa if g % 2 == 0 else pBCb
                if g == 0:
                    wt(te, P + "tpev_14")
                    wt(te, P + "xn_ready")
                    if cc > 0:
                        wt(te, f"c{cc-1}_halo")
                elif g == 1:
                    wt(te, P + "tpev_15")
                else:
                    wt(te, P + f"silu_{g-2}")
                if g == 0 and cc != 1:
                    wt(te, P + "dcv")
                for k in range(DCONV):
                    kk = DCONV - 1 - k
                    _last = te.matmul(
                        pk[:],
                        s_w2s[:, (g * DCONV + kk) * 128:
                              (g * DCONV + kk + 1) * 128],
                        xn3[:, g, 3 - k:3 - k + TC],
                        start=(k == 0), stop=(k == DCONV - 1))
                tick("te", _last)
                mark(P + f"cv_{g}", "te")

        def em_silu(sc, cc, gs):
            P = f"c{cc}_"
            for g in gs:
                pk = pBCa if g % 2 == 0 else pBCb
                wt(sc, P + f"cv_{g}")
                if cc > 0:
                    wt(sc, f"c{cc-1}_yg_{g}")
                tick("sc", sc.activation(
                    s_u16[:, g * TC:(g + 1) * TC], pk[:], AF.Silu,
                    bias=s_cols[:, OF_CONVB + g:OF_CONVB + g + 1]))
                mark(P + f"silu_{g}", "sc")

        def em_btct(te, cc):
            P = f"c{cc}_"
            for g in range(G):
                wt(te, P + f"silu_{g}")
                if g == 0:
                    wt(te, P + "silu_6")
                te.matmul(pBCa[0:NST, :],
                          s_BpCpT[:, g * NST:(g + 1) * NST],
                          s_u16[:, g * TC:(g + 1) * TC],
                          start=(g == 0), stop=(g == G - 1))
            for g in range(G):
                _last = te.matmul(pBCb[0:NST, :],
                                  s_BpCpT[:, (G + g) * NST:
                                          (G + g + 1) * NST],
                                  s_u16[:, g * TC:(g + 1) * TC],
                                  start=(g == 0), stop=(g == G - 1))
            tick("te", _last)
            mark(P + "btct_mm", "te")

        def em_btev(ve, cc):
            P = f"c{cc}_"
            wt(ve, P + "btct_mm")
            if cc > 0:
                wt(ve, f"c{cc-1}_bcmm_31")
            ve.tensor_scalar(s_bt[0:NST, 0:TC], pBCa[0:NST, :],
                             s_bpb[:, 0:1], None, ALU.add)
            tick("ve", ve.tensor_scalar(s_bt[0:NST, TC:2 * TC],
                                        pBCb[0:NST, :],
                                        s_bpb[:, 1:2], None, ALU.add))
            mark(P + "btct_ev", "ve")

        def em_bcmm(te, cc, gis):
            P = f"c{cc}_"
            for gi in gis:
                n, which = gi // 2, gi % 2
                pk = pBCa if gi % 2 == 0 else pBCb
                if gi < 2:
                    wt(te, P + "btct_ev")
                else:
                    wt(te, P + f"bcev_{gi-2}")
                tick("te", te.matmul(
                    pk[:], s_sel[:, n * 128:(n + 1) * 128],
                    s_bt[0:NST, which * TC:(which + 1) * TC],
                    start=True, stop=True))
                mark(P + f"bcmm_{gi}", "te")

        def em_bcev(sc, cc):
            P = f"c{cc}_"
            if cc > 0:
                wt(sc, f"c{cc-1}_z_{G * NST - 1}")
            for gi in range(2 * NST):
                n, which = gi // 2, gi % 2
                pk = pBCa if gi % 2 == 0 else pBCb
                wt(sc, P + f"bcmm_{gi}")
                tick("sc", sc.copy(
                    s_BC[:, (which * NST + n) * TC:
                         (which * NST + n + 1) * TC], pk[:]))
                mark(P + f"bcev_{gi}", "sc")

        def em_gate(te, cc):
            P = f"c{cc}_"
            for o in range(G):
                wt(te, P + f"gs_{o}")
                wt(te, P + "xn_ready")
                if o == 0:
                    wt(te, f"c{cc-1}_res_6")
                elif o == 1:
                    wt(te, f"c{cc-1}_res_7")
                else:
                    wt(te, P + f"gate_ev_{o-2}")
                pk = pMA if o % 2 == 0 else pMB
                xn3 = xn3S(cc)
                for k in range(G):
                    _last = te.matmul(
                        pk[:],
                        s_gs[:, (o % 2) * 1024 + k * 128:
                             (o % 2) * 1024 + (k + 1) * 128],
                        xn3[:, k, 3:3 + TC],
                        start=(k == 0), stop=(k == G - 1))
                tick("te", _last)
                mark(P + f"gate_mm_{o}", "te")

        def em_gateev(sc, cc):
            P = f"c{cc}_"
            pw = mkpw(cc)
            pw(sc, "out_mm_7")
            for o in range(G):
                pk = pMA if o % 2 == 0 else pMB
                wt(sc, P + f"gate_mm_{o}")
                tick("sc", sc.activation(
                    s_ga[:, o * TC:(o + 1) * TC], pk[:], AF.Silu,
                    bias=s_cols[:, OF_GATEB + o:OF_GATEB + o + 1]))
                mark(P + f"gate_ev_{o}", "sc")

        def em_f1(te, cc, j):
            P = f"c{cc}_"
            wt(te, P + f"w1_{j}")
            if j == 0:
                wt(te, P + "xq_ready")
                if cc > 0:
                    wt(te, f"c{cc-1}_ffn_h1_30")
            elif j == 1:
                if cc > 0:
                    wt(te, f"c{cc-1}_ffn_h1_31")
            else:
                wt(te, P + f"ffn_h1_{j-2}")
            bk = pTA if j % 2 == 0 else pTB
            xq = xqS(cc)
            for k in range(G):
                _last = te.matmul(bk[:],
                                  s_w1s[:, (j % 2) * 1024 + k * 128:
                                        (j % 2) * 1024 + (k + 1) * 128],
                                  xq[:, k * TC:(k + 1) * TC],
                                  start=(k == 0), stop=(k == G - 1))
            tick("te", _last)
            mark(P + f"ffn_mm1_{j}", "te")

        def em_h1(sc, cc, j):
            P = f"c{cc}_"
            if j == 0 and cc > 0:
                wt(sc, f"c{cc-1}_ln2_bc")
            wt(sc, P + f"ffn_mm1_{j}")
            bk = pTA if j % 2 == 0 else pTB
            tick("sc", sc.activation(t1S(j), bk[:], AF.Relu,
                                     bias=s_b1qC[:, j:j + 1],
                                     scale=1.0 / 64.0))
            mark(P + f"ffn_h1_{j}", "sc")

        def em_h3(sc, cc, j):
            P = f"c{cc}_"
            if j == 0 and cc > 0:
                wt(sc, f"c{cc-1}_ffn_mm2_{G-1}")
            wt(sc, P + f"gpclip_{j}")
            tick("sc", sc.activation(
                s_hq[:, j * TC:(j + 1) * TC], t1S(j), AF.Copy,
                bias=-RCONST))
            mark(P + f"ffn_h3_{j}", "sc")

        def em_faddev(sc, cc, os_):
            P = f"c{cc}_"
            for o in os_:
                pk = (pMA, pMB, pBCa, pBCb)[o % 4]
                wt(sc, P + f"ffn_mm2_{o}")
                tick("sc", sc.activation(
                    s_zres[:, o * TC:(o + 1) * TC], pk[:], AF.Copy,
                    scale=1.0 / 1024.0))
                mark(P + f"ffn_fadd_{o}", "sc")

        def mk_f2jobs(te, cc):
            P = f"c{cc}_"
            jobs = []
            for o in range(G):
                pk = (pMA, pMB, pBCa, pBCb)[o % 4]

                def mk_start(o=o, pk=pk):
                    def f():
                        wt(te, P + f"w2_{o}")
                        if o == 0:
                            if cc + 1 < NCH:
                                wt(te, f"c{cc+1}_gate_ev_6")
                            else:
                                wt(te, P + "res_6")
                            wt(te, P + f"ffn_h3_{HT-1}")
                        elif o == 1:
                            if cc + 1 < NCH:
                                wt(te, f"c{cc+1}_gate_ev_7")
                            else:
                                wt(te, P + "res_7")
                        elif o in (2, 3):
                            wt(te, P + "attn_ready")
                        else:
                            wt(te, P + f"ffn_fadd_{o-4}")
                        te.matmul(pk[:], s_w2s[:, (o % 2) * 4096:
                                                (o % 2) * 4096 + 128],
                                  s_hq[:, 0:TC], start=True, stop=False)
                    return f

                jobs.append(mk_start())
                for k in range(1, HT):
                    def mk_mid(o=o, k=k, pk=pk):
                        def f():
                            te.matmul(
                                pk[:],
                                s_w2s[:, (o % 2) * 4096 + k * 128:
                                      (o % 2) * 4096 + (k + 1) * 128],
                                s_hq[:, k * TC:(k + 1) * TC],
                                start=False, stop=False)
                        return f
                    jobs.append(mk_mid())

                def mk_seed(o=o, pk=pk):
                    def f():
                        wt(te, P + "attn_ready")
                        tick("te", te.matmul(
                            pk[:], s_id32[:],
                            s_zres[:, o * TC:(o + 1) * TC],
                            start=False, stop=True))
                        mark(P + f"ffn_mm2_{o}", "te")
                    return f
                jobs.append(mk_seed())
            return jobs

        def em_fint(te, cc, gis):
            P = f"c{cc}_"
            for gi in gis:
                q, half = gi // 2, gi % 2
                pk = pMA if gi % 2 == 0 else pMB
                if gi == 0:
                    wt(te, P + "ffn_fadd_4")
                elif gi == 1:
                    wt(te, P + "ffn_fadd_5")
                else:
                    wt(te, P + f"fin_ev_{gi-2}")
                for gg in range(4):
                    g = half * 4 + gg
                    wt(te, P + f"ffn_fadd_{g}")
                    _last = te.transpose(
                        pk[:, gg * 128:(gg + 1) * 128],
                        s_zres[:, g * TC + q * 128:g * TC + (q + 1) * 128],
                        s_id32[:])
                tick("te", _last)
                mark(P + f"fin_tp_{gi}", "te")

        def em_finev(sc, cc, gis):
            P = f"c{cc}_"
            for gi in gis:
                pk = pMA if gi % 2 == 0 else pMB
                wt(sc, P + f"fin_tp_{gi}")
                if gi == 0:
                    if cc > 0:
                        wt(sc, f"c{cc-1}_store_7")
                else:
                    wt(sc, P + f"store_{gi-1}")
                tick("sc", sc.copy(fstgS(gi), pk[:]))
                mark(P + f"fin_ev_{gi}", "sc")

        def em_store(gp, cc, gis):
            P = f"c{cc}_"
            t0 = cc * TC
            for gi in gis:
                q, half = gi // 2, gi % 2
                wt(gp, P + f"fin_ev_{gi}")
                tick(f"st{gi % 2}", gp.dma_start(
                    out=dout[t0 + q * 128:t0 + (q + 1) * 128,
                             half * 512:(half + 1) * 512],
                    in_=fstgS(gi), accum_op=ALU.add), 16)
                mark(P + f"store_{gi}", f"st{gi % 2}")

        def em_out(te, cc, o):
            P = f"c{cc}_"
            wt(te, P + f"os_{o}")
            if o == 0:
                for g in range(G):
                    wt(te, P + f"yg_{g}")
                if cc > 0:
                    wt(te, f"c{cc-1}_fin_ev_6")
            elif o == 1:
                if cc > 0:
                    wt(te, f"c{cc-1}_fin_ev_7")
            else:
                wt(te, P + f"res_{o-2}")
            pk = pMA if o % 2 == 0 else pMB
            for k in range(G):
                _last = te.matmul(
                    pk[:],
                    s_os[:, (o % 2) * 1024 + k * 128:
                         (o % 2) * 1024 + (k + 1) * 128],
                    s_ga[:, k * TC:(k + 1) * TC],
                    start=(k == 0), stop=(k == G - 1))
            tick("te", _last)
            mark(P + f"out_mm_{o}", "te")

        def em_halo(ve, cc):
            P = f"c{cc}_"
            src3 = xn3S(cc)
            dst3 = xn3S(cc + 1)
            if (cc + 1) % (NCH // BLOC) != 0:
                tick("ve", ve.tensor_copy(dst3[:, :, 0:3],
                                          src3[:, :, TC:TC + 3]))
            else:
                tick("ve", ve.memset(dst3[:, :, 0:3], 0.0))
            mark(P + "halo", "ve")

        def em_vclip(ve, cc, js):
            P = f"c{cc}_"
            for j in js:
                wt(ve, P + f"ffn_h1_{j}")
                tick("ve", ve.tensor_scalar(
                    t1S(j), t1S(j), RCONST, RCONST + 127.0,
                    ALU.add, ALU.min))
                mark(P + f"gpclip_{j}", "ve")

        def em_yg(ve, cc, g):
            P = f"c{cc}_"
            pk = pYA if g % 2 == 0 else pYB
            wt(ve, P + f"ymm_{g * NST + NST - 1}")
            wt(ve, P + f"gate_ev_{g}")
            tick("ve", ve.tensor_tensor(
                s_ga[:, g * TC:(g + 1) * TC], pk[:],
                s_ga[:, g * TC:(g + 1) * TC], ALU.mult))
            mark(P + f"yg_{g}", "ve")

        @block.sync
        def _(sy):
            tick("pf", sy.dma_start(out=dout[:], in_=dx[:]), 16)
            mark("prefill", "pf")
            barrier(sy)
            tick("dc", sy.dma_start(out=s_w2s[:, 0:G * DCONV * 128],
                                    in_=scr_dc[:, :]), 16)
            mark("c0_dcv", "dc")
            for q in range(NSUB):
                tick("x", sy.dma_start(
                    out=s_xtm[:, q * D:(q + 1) * D],
                    in_=dx[q * 128:(q + 1) * 128, :]), 16)
            mark("c0_x", "x")
            wt(sy, "c0_ln1_sq")
            wt(sy, "c0_ln1_ap")
            wt(sy, "c0_tp_xn_3_1")
            wt(sy, "c0_xq1_3")
            for q in range(NSUB):
                tick("x", sy.dma_start(
                    out=s_xtm[:, q * D:(q + 1) * D],
                    in_=dx[TC + q * 128:TC + (q + 1) * 128, :]), 16)
            mark("c1_x", "x")
            for o in range(G):
                if o >= 2:
                    wt(sy, f"c0_gate_mm_{o-2}")
                tick(f"gs{o % 2}", sy.dma_start(
                    out=s_gs[:, (o % 2) * 1024:(o % 2 + 1) * 1024],
                    in_=scr_g[o, :, :]), 16)
                mark(f"c0_gs_{o}", f"gs{o % 2}")
            for j in range(HT):
                if j >= 2:
                    wt(sy, f"c0_ffn_mm1_{j-2}")
                tick(f"w1{j % 2}", sy.dma_start(
                    out=s_w1s[:, (j % 2) * 1024:(j % 2 + 1) * 1024],
                    in_=scr_w1[j, :, :]), 16)
                mark(f"c0_w1_{j}", f"w1{j % 2}")

        @block.scalar
        def _(sc):
            barrier(sc)
            em_dum(sc, 0)

        @block.vector
        def _(ve):
            barrier(ve)
            em_ln1a(ve, 0)

        @block.scalar
        def _(sc):
            em_ln1sd(sc, 0)

        @block.vector
        def _(ve):
            em_ln1b(ve, 0)

        @block.scalar
        def _(sc):
            em_xq1(sc, 0, 0)
            em_xq1(sc, 0, 1)
            em_fev(sc, 0, range(0, 10))
            em_xq1(sc, 0, 2)
            em_fev(sc, 0, range(10, 12))
            em_xq1(sc, 0, 3)
            em_fev(sc, 0, range(12, 16))

        @block.vector
        def _(ve):
            for q in range(NSUB):
                em_clip(ve, 0, q)

        @block.tensor
        def _(te):
            barrier(te)
            em_fet(te, 0, range(16))

        @block.tensor
        def _(te):
            em_conv(te, 0, range(G))

        @block.scalar
        def _(sc):
            em_silu(sc, 0, range(G))

        @block.tensor
        def _(te):
            em_btct(te, 0)

        @block.vector
        def _(ve):
            em_btev(ve, 0)

        @block.tensor
        def _(te):
            em_bcmm(te, 0, range(2 * NST))

        @block.scalar
        def _(sc):
            em_bcev(sc, 0)

        @block.tensor
        def _(te):
            P = "c0_"
            for o in range(G):
                wt(te, P + f"gs_{o}")
                wt(te, P + "xn_ready")
                if o >= 2:
                    wt(te, P + f"gate_ev_{o-2}")
                pk = pMA if o % 2 == 0 else pMB
                xn3 = xn3S(0)
                for k in range(G):
                    _last = te.matmul(
                        pk[:],
                        s_gs[:, (o % 2) * 1024 + k * 128:
                             (o % 2) * 1024 + (k + 1) * 128],
                        xn3[:, k, 3:3 + TC],
                        start=(k == 0), stop=(k == G - 1))
                tick("te", _last)
                mark(P + f"gate_mm_{o}", "te")

        for c in range(NCH):
            P = f"c{c}_"
            P1 = f"c{c-1}_"
            nxt = c + 1 < NCH
            seq_start = (c % (NCH // BLOC) == 0)

            @block.sync
            def _(sy, c=c, P=P, nxt=nxt):
                pw = mkpw(c)
                def em_os(o):
                    if o >= 2:
                        wt(sy, P + f"out_mm_{o-2}")
                    else:
                        pw(sy, f"out_mm_{o + 6}")
                    tick(f"os{o % 2}", sy.dma_start(
                        out=s_os[:, (o % 2) * 1024:(o % 2 + 1) * 1024],
                        in_=scr_o[o, :, :]), 16)
                    mark(P + f"os_{o}", f"os{o % 2}")

                def em_w1(j):
                    Pn = f"c{c+1}_"
                    if j >= 2:
                        wt(sy, Pn + f"ffn_mm1_{j-2}")
                    else:
                        wt(sy, P + f"ffn_mm1_{j + 30}")
                    tick(f"w1{j % 2}", sy.dma_start(
                        out=s_w1s[:, (j % 2) * 1024:(j % 2 + 1) * 1024],
                        in_=scr_w1[j, :, :]), 16)
                    mark(Pn + f"w1_{j}", f"w1{j % 2}")

                def em_w2(o):
                    if o == 0 and nxt:
                        wt(sy, f"c{c+1}_cv_7")
                    if o >= 2:
                        wt(sy, P + f"ffn_mm2_{o-2}")
                    else:
                        pw(sy, f"ffn_mm2_{o + 6}")
                    tick(f"w2{o % 2}", sy.dma_start(
                        out=s_w2s[:, (o % 2) * 4096:(o % 2 + 1) * 4096],
                        in_=scr_w2[o, :, :]), 16)
                    mark(P + f"w2_{o}", f"w2{o % 2}")

                for o in range(G):
                    em_os(o)
                if nxt:
                    Pn = f"c{c+1}_"
                    for o in range(G):
                        if o >= 2:
                            wt(sy, Pn + f"gate_mm_{o-2}")
                        else:
                            wt(sy, P + f"gate_mm_{o + 6}")
                        tick(f"gs{o % 2}", sy.dma_start(
                            out=s_gs[:, (o % 2) * 1024:(o % 2 + 1) * 1024],
                            in_=scr_g[o, :, :]), 16)
                        mark(Pn + f"gs_{o}", f"gs{o % 2}")
                    em_w1(0)
                    em_w1(1)
                    for o in range(G):
                        em_w2(o)
                    for j in range(2, HT):
                        em_w1(j)
                else:
                    for o in range(G):
                        em_w2(o)

            @block.tensor
            def _(te, c=c, P=P, P1=P1, nxt=nxt):
                f2jobs = mk_f2jobs(te, c - 1) if c >= 1 else []
                ji = 0
                for g in range(G):
                    pk = pYA if g % 2 == 0 else pYB
                    if g < 2:
                        if c > 0:
                            wt(te, P1 + f"yg_{6 + g}")
                    else:
                        wt(te, P + f"yg_{g-2}")
                    te.matmul(pk[:], s_diagD[:, g * 128:(g + 1) * 128],
                              s_u16[:, g * TC:(g + 1) * TC],
                              start=True, stop=False)
                    zslots = (s_z[:, 0:TC], s_z16[:, 0:TC],
                              s_z16[:, TC:2 * TC])
                    for n in range(NST):
                        idx = g * NST + n
                        wt(te, P + f"z_{idx}")
                        tick("te", te.matmul(
                            pk[:], s_id16[:], zslots[idx % 3],
                            start=False, stop=(n == NST - 1)))
                        mark(P + f"ymm_{idx}", "te")
                        cap = 4 * (HT + 1) if g == 0 else len(f2jobs)
                        for _ in range(9):
                            if ji < min(cap, len(f2jobs)):
                                f2jobs[ji]()
                                ji += 1
                        if n == 7 and g >= 1:
                            em_f1(te, c, 4 * (g - 1) + 2)
                            em_f1(te, c, 4 * (g - 1) + 3)
                    em_f1(te, c, 4 * g)
                    em_f1(te, c, 4 * g + 1)
                    if nxt:
                        if g == 2:
                            em_fet(te, c + 1, range(0, 4))
                        elif g == 3:
                            em_fet(te, c + 1, range(4, 10))
                        elif g == 4:
                            em_fet(te, c + 1, range(10, 12))
                        elif g == 5:
                            em_fet(te, c + 1, range(12, 14))
                        elif g == 6:
                            em_fet(te, c + 1, range(14, 16))
                            em_conv(te, c + 1, range(0, 2))
                        elif g == 7:
                            em_conv(te, c + 1, range(2, 6))
                    if c >= 1:
                        if g == 3:
                            em_fint(te, c - 1, range(0, 4))
                        elif g == 4:
                            em_fint(te, c - 1, range(4, 8))
                assert ji == len(f2jobs)
                em_f1(te, c, 4 * (G - 1) + 2)
                em_f1(te, c, 4 * (G - 1) + 3)
                if nxt:
                    em_conv(te, c + 1, range(6, 8))
                    em_btct(te, c + 1)
                for o in range(G):
                    em_out(te, c, o)
                    if nxt:
                        em_bcmm(te, c + 1, range(4 * o, 4 * o + 4))
                bP = f"c{c+1}_" if nxt else P
                for g in range(G):
                    par = g % 2
                    wt(te, P + f"ln2cast_{g}")
                    if g == 0:
                        wt(te, bP + "bcev_30")
                    tick("te", te.matmul(
                        pBCa[0:1, :], s_ones16[:],
                        s_z16[:, par * TC:(par + 1) * TC],
                        start=(g == 0), stop=(g == G - 1)))
                    mark(P + f"ln2mu_{g}", "te")
                    wt(te, P + f"ln2sqc_{g}")
                    if g == 0:
                        wt(te, bP + "bcev_31")
                    tick("te", te.matmul(
                        pBCb[0:1, :], s_ones16[:],
                        s_z16[:, par * TC:(par + 1) * TC],
                        start=(g == 0), stop=(g == G - 1)))
                    mark(P + f"ln2sq_{g}", "te")
                wt(te, P + "ln2_r")
                te.matmul(pBCa[:], s_ones1[:], VCH, start=True, stop=True)
                tick("te", te.matmul(pBCb[:], s_ones1[:], MUR,
                                     start=True, stop=True))
                mark(P + "ln2_bc", "te")
                if nxt:
                    em_gate(te, c + 1)

            @block.scalar
            def _(sc, c=c, P=P, P1=P1, nxt=nxt):
                if nxt:
                    em_dum(sc, c + 1)
                em_gateev(sc, c)

                def inj(j):
                    if j == 1:
                        if nxt:
                            em_ln1sd(sc, c + 1)
                        if c >= 1:
                            em_faddev(sc, c - 1, (0, 1, 2, 3))
                    elif j == 3:
                        if nxt:
                            em_xq1(sc, c + 1, 0)
                        if c >= 1:
                            em_faddev(sc, c - 1, (4, 5))
                    elif j == 5:
                        if c >= 1:
                            em_faddev(sc, c - 1, (6, 7))
                    elif j == 7:
                        if nxt:
                            em_fev(sc, c + 1, range(0, 4))
                    elif j == 9:
                        if nxt:
                            em_xq1(sc, c + 1, 1)
                    elif j == 11:
                        if nxt:
                            em_fev(sc, c + 1, range(4, 8))
                        if c >= 1:
                            em_finev(sc, c - 1, range(0, 2))
                    elif j == 13:
                        if nxt:
                            em_fev(sc, c + 1, range(8, 10))
                            em_xq1(sc, c + 1, 2)
                        if c >= 1:
                            em_finev(sc, c - 1, range(2, 4))
                    elif j == 15:
                        if c >= 1:
                            em_finev(sc, c - 1, range(4, 8))
                    elif j == 17:
                        if nxt:
                            em_fev(sc, c + 1, range(10, 12))
                            em_xq1(sc, c + 1, 3)
                    elif j == 21:
                        if nxt:
                            em_fev(sc, c + 1, range(12, 14))
                    elif j == 25:
                        if nxt:
                            em_fev(sc, c + 1, range(14, 16))
                            em_silu(sc, c + 1, (0, 1))
                    elif j == 29:
                        if nxt:
                            em_silu(sc, c + 1, (2, 3, 4, 5, 6, 7))

                for j in range(HT):
                    if j >= 4:
                        em_h3(sc, c, j - 4)
                    em_h1(sc, c, j)
                    inj(j)
                for j in range(HT - 4, HT):
                    em_h3(sc, c, j)
                if nxt:
                    em_bcev(sc, c + 1)
                for g in range(G):
                    par = g % 2
                    wt(sc, P + f"res_{g}")
                    if g >= 2:
                        wt(sc, P + f"ln2sq_{g-2}")
                    tick("sc", sc.copy(s_z16[:, par * TC:(par + 1) * TC],
                                       s_zres[:, g * TC:(g + 1) * TC]))
                    mark(P + f"ln2cast_{g}", "sc")
                    wt(sc, P + f"ln2mu_{g}")
                    tick("sc", sc.activation(
                        s_z16[:, par * TC:(par + 1) * TC],
                        s_z16[:, par * TC:(par + 1) * TC], AF.Square))
                    mark(P + f"ln2sqc_{g}", "sc")
                wt(sc, P + "ln2_var")
                tick("sc", sc.activation(VCH, VCH, AF.Sqrt))
                mark(P + "ln2_sd", "sc")

            @block.vector
            def _(ve, c=c, P=P, P1=P1, nxt=nxt, seq_start=seq_start):
                if nxt:
                    em_halo(ve, c)
                zslots = (s_z[:, 0:TC], s_z16[:, 0:TC], s_z16[:, TC:2 * TC])
                for g in range(G):
                    for n in range(NST):
                        idx = g * NST + n
                        half = n % 2
                        zsl = zslots[idx % 3]
                        if idx >= 3:
                            wt(ve, P + f"ymm_{idx-3}")
                        if g == 0:
                            wt(ve, P + f"bcev_{2 * n}")
                            wt(ve, P + f"bcev_{2 * n + 1}")
                        ve.tensor_tensor(s_w[:],
                                         s_u16[:, g * TC:(g + 1) * TC],
                                         s_BC[:, n * TC:(n + 1) * TC],
                                         ALU.mult)
                        ci = g * NST + n
                        init = (0.0 if seq_start
                                else s_carry[:, ci:ci + 1])
                        ve.tensor_tensor_scan(
                            s_s[:, half * TC:(half + 1) * TC],
                            s_acol[:, ci:ci + 1].broadcast_to([128, TC]),
                            s_w[:], init, ALU.mult, ALU.add)
                        tick("ve", ve.tensor_tensor(
                            zsl,
                            s_s[:, half * TC:(half + 1) * TC],
                            s_BC[:, (NST + n) * TC:(NST + n + 1) * TC],
                            ALU.mult))
                        mark(P + f"z_{idx}", "ve")
                        ve.tensor_copy(
                            s_carry[:, ci:ci + 1],
                            s_s[:, (half + 1) * TC - 1:(half + 1) * TC])
                        if n == 7 and g >= 1:
                            em_vclip(ve, c, (4 * (g - 1), 4 * (g - 1) + 1))
                    if g >= 1:
                        em_yg(ve, c, g - 1)
                        em_vclip(ve, c, (4 * (g - 1) + 2, 4 * (g - 1) + 3))
                    if nxt:
                        if g == 0:
                            em_ln1a(ve, c + 1)
                        elif g == 1:
                            em_ln1b(ve, c + 1)
                        elif 2 <= g <= 5:
                            em_clip(ve, c + 1, g - 2)
                em_yg(ve, c, G - 1)
                if nxt:
                    em_btev(ve, c + 1)
                if c >= 1:
                    wt(ve, P1 + "fin_tp_7")
                xn3 = xn3S(c)
                for o in range(G):
                    pk = pMA if o % 2 == 0 else pMB
                    wt(ve, P + f"out_mm_{o}")
                    tick("ve", ve.scalar_tensor_tensor(
                        s_zres[:, o * TC:(o + 1) * TC], pk[:],
                        s_cols[:, OF_OUTB + o:OF_OUTB + o + 1],
                        xn3[:, o, 3:3 + TC],
                        ALU.add, ALU.add))
                    mark(P + f"res_{o}", "ve")
                    if o == 0:
                        em_vclip(ve, c, range(4 * (G - 1), 4 * G))

                def vst(inst):
                    tick("ve", inst)
                    ve.wait_ge(sems["ve"], L.cnt["ve"])

                wt(ve, P + f"ln2sq_{G-1}")
                wt(ve, P + "ffn_h3_28")
                wt(ve, P + "ffn_h3_29")
                vst(ve.tensor_scalar(MU, pBCa[0:1, :], 1.0 / D, None,
                                     ALU.mult))
                vst(ve.tensor_scalar(VCH, pBCb[0:1, :], 1.0 / D, None,
                                     ALU.mult))
                vst(ve.tensor_tensor(MUR, MU, MU, ALU.mult))
                vst(ve.tensor_tensor(VCH, VCH, MUR, ALU.subtract))
                tick("ve", ve.tensor_scalar(VCH, VCH, LN_EPS, None,
                                            ALU.add))
                mark(P + "ln2_var", "ve")
                wt(ve, P + "ln2_sd")
                vst(ve.reciprocal(VCH, VCH))
                tick("ve", ve.tensor_tensor(MUR, MU, VCH, ALU.mult))
                mark(P + "ln2_r", "ve")
                wt(ve, P + "ln2_bc")
                z3 = s_zres.rearrange("p (g t) -> p g t", t=TC)
                ve.tensor_tensor(z3[:, :, :], z3[:, :, :],
                                 pBCa[:, None, :].broadcast_to(
                                     [128, G, TC]), ALU.mult)
                ve.tensor_tensor(z3[:, :, :], z3[:, :, :],
                                 pBCb[:, None, :].broadcast_to(
                                     [128, G, TC]), ALU.subtract)
                for g in range(G):
                    _last = ve.tensor_scalar(
                        s_zres[:, g * TC:(g + 1) * TC],
                        s_zres[:, g * TC:(g + 1) * TC],
                        s_cols[:, OF_LNG + g:OF_LNG + g + 1],
                        s_cols[:, OF_LNB2 + g:OF_LNB2 + g + 1],
                        ALU.mult, ALU.add)
                tick("ve", _last)
                mark(P + "attn_ready", "ve")

            @block.gpsimd
            def _(gp, c=c, P=P):
                wt(gp, "prefill")
                if c >= 1 and c + 1 < NCH:
                    wt(gp, f"c{c-1}_ffn_mm2_6")
                    tick("dcg", gp.dma_start(
                        out=s_w2s[:, 0:G * DCONV * 128],
                        in_=scr_dc[:, :]), 16)
                    mark(f"c{c+1}_dcv", "dcg")
                if c >= 1:
                    em_store(gp, c - 1, range(0, 8))
                if c + 2 < NCH:
                    Pn = f"c{c+1}_"
                    wt(gp, Pn + "ln1_sq")
                    wt(gp, Pn + "ln1_ap")
                    wt(gp, Pn + "tp_xn_3_1")
                    wt(gp, Pn + "xq1_3")
                    t0n = (c + 2) * TC
                    for q in range(NSUB):
                        tick("xg", gp.dma_start(
                            out=s_xtm[:, q * D:(q + 1) * D],
                            in_=dx[t0n + q * 128:
                                   t0n + (q + 1) * 128, :]), 16)
                    mark(f"c{c+2}_x", "xg")

            for k in ("gp", "te", "ve", "sc", "st0", "st1"):
                mark(P + k, k)

        cl = NCH - 1
        Pl = f"c{cl}_"

        @block.tensor
        def _(te):
            f2jobs = mk_f2jobs(te, cl)
            for f in f2jobs:
                f()
            em_fint(te, cl, range(8))

        @block.scalar
        def _(sc):
            em_faddev(sc, cl, range(G))
            em_finev(sc, cl, range(8))

        @block.gpsimd
        def _(gp):
            em_store(gp, cl, range(8))

        if DEBUG_TAPS:
            @block.gpsimd
            def _(gp):
                for k in ("gp", "te", "ve", "sc"):
                    wt(gp, f"c{NCH-1}_" + k)
                pairs = [("xn16", s_xn16), ("u16d", s_u16), ("btd", s_bt),
                         ("BCd", s_BC), ("xqd", s_xq), ("hqd", s_hq),
                         ("gad", s_ga), ("zresd", s_zres),
                         ("carryd", s_carry), ("acold", s_acol),
                         ("t1d", s_t1), ("statd", s_stat),
                         ("xtmd", s_xtm)]
                for nm, t in pairs:
                    tick("st0", gp.dma_start(out=dbg[nm], in_=t[:]), 16)
                tick("st0", gp.dma_start(out=dbg["scrw1d"],
                                        in_=scr_w1[5, :, :]), 16)
                tick("st0", gp.dma_start(out=dbg["scrw2d"],
                                        in_=scr_w2[3, :, :]), 16)

        @block.gpsimd
        def _(gp):
            if L.prev is not None:
                for k in ("st0", "st1", "st2", "st3"):
                    gp.wait_ge(sems[k], L.cnt[k])

    return nc


def build_program():
    led1 = Ledger(None)
    nc1 = bass.Bass("TRN2", target_bir_lowering=False,
                    detect_race_conditions=False)
    emit(nc1, led1)
    led2 = Ledger(led1.marks)
    nc2 = bass.Bass("TRN2", target_bir_lowering=False,
                    detect_race_conditions=False)
    emit(nc2, led2)
    assert led1.cnt == led2.cnt, (led1.cnt, led2.cnt)
    assert led1.marks == led2.marks
    return nc2


_NC_CACHE = None


def kernel(**inputs):
    global _NC_CACHE
    if _NC_CACHE is None:
        _NC_CACHE = build_program()
    nc = _NC_CACHE

    x = np.ascontiguousarray(inputs["L0_attn_x"], dtype=np.float32)
    base = {
        "conv_w": np.ascontiguousarray(
            inputs["L0_attn_conv_w"].reshape(D, DCONV), np.float32),
        "conv_b": np.ascontiguousarray(inputs["L0_attn_conv_b"], np.float32),
        "A": np.ascontiguousarray(inputs["L0_attn_A"], np.float32),
        "Bp_w": np.ascontiguousarray(inputs["L0_attn_Bp_w"], np.float32),
        "Bp_b": np.ascontiguousarray(inputs["L0_attn_Bp_b"], np.float32),
        "Cp_w": np.ascontiguousarray(inputs["L0_attn_Cp_w"], np.float32),
        "Cp_b": np.ascontiguousarray(inputs["L0_attn_Cp_b"], np.float32),
        "Dv": np.ascontiguousarray(inputs["L0_attn_D"], np.float32),
        "gate_w": np.ascontiguousarray(inputs["L0_attn_gate_w"], np.float32),
        "gate_b": np.ascontiguousarray(inputs["L0_attn_gate_b"], np.float32),
        "out_w": np.ascontiguousarray(inputs["L0_attn_out_w"], np.float32),
        "out_b": np.ascontiguousarray(inputs["L0_attn_out_b"], np.float32),
        "ln_g": np.ascontiguousarray(inputs["L0_attn_ln_g"], np.float32),
        "ln_b": np.ascontiguousarray(inputs["L0_attn_ln_b"], np.float32),
        "w1": np.ascontiguousarray(inputs["L0_ffn_w1"], np.float32),
        "b1": np.ascontiguousarray(inputs["L0_ffn_b1"], np.float32),
        "w2": np.ascontiguousarray(inputs["L0_ffn_w2"], np.float32),
        "b2": np.ascontiguousarray(inputs["L0_ffn_b2"], np.float32),
    }
    in_maps = []
    for ci in range(NCORES):
        m = dict(base)
        m["x"] = np.ascontiguousarray(
            x[ci * BLOC:(ci + 1) * BLOC].reshape(T, D))
        in_maps.append(m)

    res = run_bass_kernel_spmd(nc, in_maps, core_ids=list(range(NCORES)))
    out = np.empty((16, SEQ, D), np.float32)
    for ci in range(NCORES):
        out[ci * BLOC:(ci + 1) * BLOC] = res.results[ci]["out"].reshape(
            BLOC, SEQ, D)
    return out


# version 17
# speedup vs baseline: 1.0148x; 1.0038x over previous
import contextlib
import numpy as np

import concourse.bass as bass
import concourse.mybir as mybir
from concourse.bass_utils import run_bass_kernel_spmd

F32 = mybir.dt.float32
BF16 = mybir.dt.bfloat16
AF = mybir.ActivationFunctionType
ALU = mybir.AluOpType
AX = mybir.AxisListType

NCORES = 8
SEQ = 2048
BLOC = 2
T = SEQ * BLOC
D = 1024
H = 4096
NST = 16
DCONV = 4
LN_EPS = 1e-5
G = D // 128
HT = H // 128
TC = 512
NCH = T // TC
NSUB = TC // 128
RCONST = float(3 << 22)
DEBUG_TAPS = False


SEMKEYS = ("sy", "gp", "te", "ve", "sc", "params", "x", "xg", "pf", "dc", "dcg",
           "st0", "st1", "st2", "st3",
           "gs0", "gs1", "os0", "os1", "w10", "w11", "w20", "w21",
           "wp0", "wp1", "scr0", "scr1")


class Ledger:
    def __init__(self, prev_marks):
        self.cnt = {k: 0 for k in SEMKEYS}
        self.marks = {}
        self.prev = prev_marks

    def tick(self, sems, key, inst, k=1):
        inst.then_inc(sems[key], k)
        self.cnt[key] += k

    def mark(self, name, key):
        assert name not in self.marks, name
        self.marks[name] = (key, self.cnt[key])

    def wait(self, sems, eng, name):
        if self.prev is None:
            return
        key, val = self.prev[name]
        if val > 0:
            eng.wait_ge(sems[key], val)


def emit(nc, led):
    L = led

    dx = nc.dram_tensor("x", [T, D], F32, kind="ExternalInput").ap()
    dconv_w = nc.dram_tensor("conv_w", [D, DCONV], F32, kind="ExternalInput").ap()
    dconv_b = nc.dram_tensor("conv_b", [D], F32, kind="ExternalInput").ap()
    dA = nc.dram_tensor("A", [D, NST], F32, kind="ExternalInput").ap()
    dBpw = nc.dram_tensor("Bp_w", [NST, D], F32, kind="ExternalInput").ap()
    dBpb = nc.dram_tensor("Bp_b", [NST], F32, kind="ExternalInput").ap()
    dCpw = nc.dram_tensor("Cp_w", [NST, D], F32, kind="ExternalInput").ap()
    dCpb = nc.dram_tensor("Cp_b", [NST], F32, kind="ExternalInput").ap()
    dD = nc.dram_tensor("Dv", [D], F32, kind="ExternalInput").ap()
    dgw = nc.dram_tensor("gate_w", [D, D], F32, kind="ExternalInput").ap()
    dgb = nc.dram_tensor("gate_b", [D], F32, kind="ExternalInput").ap()
    dow = nc.dram_tensor("out_w", [D, D], F32, kind="ExternalInput").ap()
    dob = nc.dram_tensor("out_b", [D], F32, kind="ExternalInput").ap()
    dlng = nc.dram_tensor("ln_g", [D], F32, kind="ExternalInput").ap()
    dlnb = nc.dram_tensor("ln_b", [D], F32, kind="ExternalInput").ap()
    dw1 = nc.dram_tensor("w1", [H, D], F32, kind="ExternalInput").ap()
    db1 = nc.dram_tensor("b1", [H], F32, kind="ExternalInput").ap()
    dw2 = nc.dram_tensor("w2", [D, H], F32, kind="ExternalInput").ap()
    db2 = nc.dram_tensor("b2", [D], F32, kind="ExternalInput").ap()
    dout = nc.dram_tensor("out", [T, D], F32, kind="ExternalOutput").ap()
    scr_w1 = nc.dram_tensor("scr_w1", [HT, 128, 1024], BF16, kind="Internal").ap()
    scr_w2 = nc.dram_tensor("scr_w2", [G, 128, 4096], BF16, kind="Internal").ap()
    scr_g = nc.dram_tensor("scr_g", [G, 128, 1024], BF16, kind="Internal").ap()
    scr_o = nc.dram_tensor("scr_o", [G, 128, 1024], BF16, kind="Internal").ap()
    scr_dc = nc.dram_tensor("scr_dc", [128, G * DCONV * 128], BF16,
                            kind="Internal").ap()
    dbg = {}
    if DEBUG_TAPS:
        for nm, shape, dt in (
                ("xn16", [128, G * (3 + TC)], BF16),
                ("u16d", [128, G * TC], BF16),
                ("btd", [NST, 2 * TC], BF16),
                ("BCd", [128, 2 * NST * TC], BF16),
                ("xqd", [128, G * TC], BF16),
                ("hqd", [128, HT * TC], BF16),
                ("gad", [128, G * TC], BF16),
                ("zresd", [128, G * TC], F32),
                ("carryd", [128, G * NST], F32),
                ("acold", [128, G * NST], F32),
                ("t1d", [128, 3 * TC], F32),
                ("statd", [128, NSUB * 8], F32),
                ("xtmd", [128, NSUB * D], F32),
                ("scrw1d", [128, 1024], BF16),
                ("scrw2d", [128, 4096], BF16)):
            dbg[nm] = nc.dram_tensor("dbg_" + nm, shape, dt,
                                     kind="ExternalOutput").ap()

    ctx = contextlib.ExitStack()
    sb = lambda name, shape, dt=F32: ctx.enter_context(
        nc.sbuf_tensor(name, shape, dt))
    ps = lambda name: ctx.enter_context(nc.psum_tensor(name, [128, 512], F32))

    with ctx:
        s_BpCpT = sb("BpCpT", [128, 2 * G * NST], BF16)
        s_acol = sb("acol", [128, G * NST])
        s_diagD = sb("diagD", [128, G * 128], BF16)
        s_convw = sb("convw", [128, G * DCONV])
        s_cols = sb("cols", [128, 5 * G])
        OF_CONVB, OF_GATEB, OF_OUTB, OF_LNG, OF_LNB2 = (
            0, G, 2 * G, 3 * G, 4 * G)
        s_b1qC = sb("b1qC", [128, HT])
        s_sel = sb("sel", [NST, NST * 128], BF16)
        s_bpb = sb("bpb", [NST, 2])
        s_id32 = sb("id32", [128, 128])
        s_id16 = sb("id16", [128, 128], BF16)
        s_ones1 = sb("ones1", [1, 128])
        s_negC = sb("negC", [128, 1])
        s_ones16 = sb("ones16c", [128, 1], BF16)
        s_carry = sb("carry", [128, G * NST])
        s_w1s = sb("w1s", [128, 2 * 1024], BF16)
        s_w2s = sb("w2s", [128, 2 * 4096], BF16)
        s_gs = sb("gs", [128, 2 * 1024], BF16)
        s_os = sb("osb", [128, 2 * 1024], BF16)
        s_xtm = sb("xtm", [128, NSUB * D])
        s_stat = sb("stat", [128, NSUB * 8])
        s_xq16 = sb("xq16", [128, 2 * D], BF16)
        s_xqtm = sb("xqtm", [128, D])
        s_fstg = sb("fstg", [128, 512])
        s_xn16 = sb("xn16", [128, 2 * G * (3 + TC)], BF16)
        s_xq = sb("xq", [128, 2 * G * TC], BF16)
        s_u16 = sb("u16", [128, G * TC], BF16)
        s_bt = sb("bt", [NST, 2 * TC], BF16)
        s_BC = sb("BC", [128, 2 * NST * TC], BF16)
        s_w = sb("wbuf", [128, TC], BF16)
        s_s = sb("sbuf_s", [128, 2 * TC])
        s_z = sb("zbuf", [128, TC], BF16)
        s_ga = sb("ga", [128, G * TC], BF16)
        s_zres = sb("zres", [128, G * TC])
        s_z16 = sb("z16", [128, 2 * TC], BF16)
        s_hq = sb("hq", [128, HT * TC], BF16)
        s_t1 = sb("t1", [128, 3 * TC])
        pTA, pTB = ps("pTA"), ps("pTB")
        pMA, pMB = ps("pMA"), ps("pMB")
        pBCa, pBCb = ps("pBCa"), ps("pBCb")
        pYA, pYB = ps("pYA"), ps("pYB")

        sems = {k: ctx.enter_context(nc.semaphore("sem_" + k))
                for k in SEMKEYS}
        block = ctx.enter_context(nc.Block())

        def tick(key, inst, k=1):
            L.tick(sems, key, inst, k)

        def mark(name, key):
            L.mark(name, key)

        def wt(eng, name):
            L.wait(sems, eng, name)

        import contextlib as _cl

        def scoped(name):
            return nc.named_scope(name)

        @block.gpsimd
        def _(gp):
            def ss(inst):
                tick("gp", inst)
                gp.wait_ge(sems["gp"], L.cnt["gp"])

            ss(gp.memset(s_id32[:], 0.0))
            ss(nc.gpsimd.affine_select(
                out=s_id32[:], in_=s_id32[:], compare_op=ALU.not_equal,
                fill=1.0, base=0, pattern=[[-1, 128]], channel_multiplier=1))
            ss(gp.memset(s_id16[:], 0.0))
            ss(nc.gpsimd.affine_select(
                out=s_id16[:], in_=s_id16[:], compare_op=ALU.not_equal,
                fill=1.0, base=0, pattern=[[-1, 128]], channel_multiplier=1))
            ss(gp.memset(s_sel[:], 0.0))
            sel3 = s_sel.rearrange("p (n f) -> p n f", f=128)
            ss(nc.gpsimd.affine_select(
                out=sel3, in_=sel3, compare_op=ALU.not_equal,
                fill=1.0, base=0, pattern=[[-1, NST], [0, 128]],
                channel_multiplier=1))
            gp.memset(s_ones1[:], 1.0)
            gp.memset(s_negC[:], -RCONST)
            gp.memset(s_ones16[:], 1.0)
            gp.memset(s_carry[:], 0.0)
            tick("gp", gp.memset(s_xn16[:], 0.0))
            mark("gp_const", "gp")

        @block.sync
        def _(sy):
            for g in range(G):
                tick("params", sy.dma_start(
                    out=s_convw[:, g * DCONV:(g + 1) * DCONV],
                    in_=dconv_w[g * 128:(g + 1) * 128, :]), 16)
                tick("params", sy.dma_start(
                    out=s_cols[:, OF_CONVB + g:OF_CONVB + g + 1],
                    in_=dconv_b[g * 128:(g + 1) * 128][:, None]), 16)
                tick("params", sy.dma_start(
                    out=s_cols[:, OF_GATEB + g:OF_GATEB + g + 1],
                    in_=dgb[g * 128:(g + 1) * 128][:, None]), 16)
                tick("params", sy.dma_start(
                    out=s_cols[:, OF_OUTB + g:OF_OUTB + g + 1],
                    in_=dob[g * 128:(g + 1) * 128][:, None]), 16)
                tick("params", sy.dma_start(
                    out=s_cols[:, OF_LNG + g:OF_LNG + g + 1],
                    in_=dlng[g * 128:(g + 1) * 128][:, None]), 16)
                tick("params", sy.dma_start(
                    out=s_acol[:, g * NST:(g + 1) * NST],
                    in_=dA[g * 128:(g + 1) * 128, :]), 16)
                tick("params", sy.dma_start(
                    out=s_stat[:, g:g + 1],
                    in_=dD[g * 128:(g + 1) * 128][:, None]), 16)
                tick("params", sy.dma_start(
                    out=s_stat[:, 8 + g:8 + g + 1],
                    in_=dlnb[g * 128:(g + 1) * 128][:, None]), 16)
            tick("params", sy.dma_start(out=s_bpb[:, 0:1], in_=dBpb[:, None]), 16)
            tick("params", sy.dma_start(out=s_bpb[:, 1:2], in_=dCpb[:, None]), 16)
            tick("params", sy.dma_start(
                out=s_zres[0:32, 0:128],
                in_=db1.rearrange("(a b) -> a b", b=128)), 16)
            tick("params", sy.dma_start(
                out=s_zres[0:8, 200:328],
                in_=db2.rearrange("(a b) -> a b", b=128)), 16)
            tick("params", sy.dma_start(
                out=s_zres[0:NST, 400:400 + D], in_=dBpw[:, :]), 16)
            tick("params", sy.dma_start(
                out=s_zres[0:NST, 1500:1500 + D], in_=dCpw[:, :]), 16)
            mark("sy_params", "params")

        @block.scalar
        def _(sc):
            wt(sc, "sy_params")
            tick("sc", sc.activation(s_acol[:], s_acol[:], AF.Exp))
            sc.wait_ge(sems["sc"], L.cnt["sc"])
            tick("sc", sc.activation(s_acol[:], s_acol[:], AF.Exp, scale=-1.0))
            mark("sc_acol", "sc")

        @block.tensor
        def _(te):
            wt(te, "sy_params")
            wt(te, "gp_const")
            te.transpose(pTA[:, 0:32], s_zres[0:32, 0:128],
                         s_id32[0:32, 0:32])
            te.transpose(pTA[:, 32:40], s_zres[0:8, 200:328],
                         s_id32[0:8, 0:8])
            for g in range(G):
                te.transpose(pTB[:, g * 16:(g + 1) * 16],
                             s_zres[0:NST, 400 + g * 128:400 + (g + 1) * 128],
                             s_id32[0:NST, 0:NST])
            for g in range(G):
                _last = te.transpose(pTB[:, 128 + g * 16:128 + (g + 1) * 16],
                             s_zres[0:NST, 1500 + g * 128:1500 + (g + 1) * 128],
                             s_id32[0:NST, 0:NST])
            tick("te", _last)
            mark("te_prep1", "te")

        @block.vector
        def _(ve):
            def vst(inst):
                tick("ve", inst)
                ve.wait_ge(sems["ve"], L.cnt["ve"])

            wt(ve, "te_prep1")
            wt(ve, "sy_params")
            wt(ve, "gp_const")
            for g in range(G):
                ve.tensor_scalar(s_diagD[:, g * 128:(g + 1) * 128],
                                 s_id16[:], s_stat[:, g:g + 1], None, ALU.mult)
            for g in range(G):
                for k in range(DCONV):
                    i = g * DCONV + k
                    _last = ve.tensor_scalar(
                        s_BC[:, i * 128:(i + 1) * 128],
                        s_id16[:], s_convw[:, i:i + 1], None, ALU.mult)
            tick("ve", _last)
            mark("dconv_stage", "ve")
            vst(ve.tensor_scalar(s_b1qC[:], pTA[:, 0:HT], 64.0, RCONST,
                                 ALU.mult, ALU.add))
            vst(ve.tensor_scalar(s_b1qC[:], s_b1qC[:], RCONST + 127.0,
                                 RCONST - 128.0, ALU.min, ALU.max))
            vst(ve.tensor_scalar(s_b1qC[:], s_b1qC[:], RCONST, None,
                                 ALU.subtract))
            vst(ve.tensor_scalar(s_b1qC[:], s_b1qC[:], 0.25, None, ALU.mult))
            vst(ve.tensor_scalar(s_cols[:, OF_LNB2:OF_LNB2 + 8],
                                 pTA[:, 32:40], 64.0, RCONST,
                                 ALU.mult, ALU.add))
            vst(ve.tensor_scalar(s_cols[:, OF_LNB2:OF_LNB2 + 8],
                                 s_cols[:, OF_LNB2:OF_LNB2 + 8],
                                 RCONST + 127.0, RCONST - 128.0,
                                 ALU.min, ALU.max))
            vst(ve.tensor_scalar(s_cols[:, OF_LNB2:OF_LNB2 + 8],
                                 s_cols[:, OF_LNB2:OF_LNB2 + 8],
                                 RCONST, None, ALU.subtract))
            vst(ve.tensor_scalar(s_stat[:, 8:16], s_stat[:, 8:16],
                                 1024.0, None, ALU.mult))
            vst(ve.scalar_tensor_tensor(
                s_cols[:, OF_LNB2:OF_LNB2 + 8],
                s_cols[:, OF_LNB2:OF_LNB2 + 8], 16.0,
                s_stat[:, 8:16], ALU.mult, ALU.add))
            vst(ve.tensor_scalar(s_cols[:, OF_LNG:OF_LNG + 8],
                                 s_cols[:, OF_LNG:OF_LNG + 8],
                                 1024.0, None, ALU.mult))
            ve.tensor_copy(s_BpCpT[:, 0:G * NST], pTB[:, 0:G * NST])
            tick("ve", ve.tensor_copy(s_BpCpT[:, G * NST:2 * G * NST],
                                      pTB[:, 128:128 + G * NST]))
            mark("ve_prep1", "ve")

        wjobs = []
        for (src, dst, nrt, rowlen, mode) in (
                (dw1, scr_w1, HT, D, "q64"),
                (dw2, scr_w2, G, H, "q64"),
                (dgw, scr_g, G, D, "cast"),
                (dow, scr_o, G, D, "cast")):
            for j in range(nrt):
                wjobs.append((src, dst, j, rowlen, mode))
        NJ = len(wjobs)

        def wp_in(i, rowlen):
            if rowlen == H:
                return s_zres[:, 0:H]
            return s_xtm[:, (i % 2) * 2048:(i % 2) * 2048 + rowlen]

        def wp_t1(i, rowlen):
            if rowlen == H:
                return s_BC[:].bitcast(F32)[:, 0:H]
            return s_xtm[:, (i % 2) * 2048 + 1024:
                          (i % 2) * 2048 + 1024 + rowlen]

        @block.sync
        def _(sy):
            for i, (src, dst, j, rowlen, mode) in enumerate(wjobs):
                if rowlen == H:
                    wt(sy, "te_prep1")
                    if i >= 1:
                        wt(sy, f"wp_read_{i-1}")
                if i >= 2:
                    wt(sy, f"wp_read_{i-2}")
                tick(f"wp{i % 2}", sy.dma_start(
                    out=wp_in(i, rowlen),
                    in_=src[j * 128:(j + 1) * 128, :]), 16)
                mark(f"wp_dma_{i}", f"wp{i % 2}")

        @block.scalar
        def _(sc):
            for i, (src, dst, j, rowlen, mode) in enumerate(wjobs):
                par = i % 2
                wt(sc, f"wp_dma_{i}")
                src_sb = wp_in(i, rowlen)
                if mode == "q64":
                    if i >= 1:
                        wt(sc, f"wp_clip_{i-1}")
                    if i >= 2:
                        wt(sc, f"wp_clip_{i-2}")
                    tick("sc", sc.activation(
                        wp_t1(i, rowlen), src_sb,
                        AF.Copy, bias=RCONST, scale=64.0))
                else:
                    if i >= 2:
                        wt(sc, f"wp_te_{i-2}")
                    tick("sc", sc.activation(
                        s_hq[:, par * 4096:par * 4096 + rowlen], src_sb,
                        AF.Copy))
                mark(f"wp_read_{i}", "sc")



        @block.tensor
        def _(te):
            gidx = 0
            for i, (src, dst, j, rowlen, mode) in enumerate(wjobs):
                par = i % 2
                if mode == "q64":
                    wt(te, f"wp_clip_{i}")
                else:
                    wt(te, f"wp_read_{i}")
                if i >= 2:
                    wt(te, f"wp_stage_{i-2}")
                int_sb = s_hq[:, par * 4096:par * 4096 + rowlen]
                nk = rowlen // 128
                for k0 in range(0, nk, 4):
                    pk = pTA if gidx % 2 == 0 else pTB
                    pk16 = pk[:].bitcast(BF16)
                    if gidx < 2:
                        wt(te, "ve_prep1")
                    if gidx >= 2:
                        wt(te, f"wp_ev_{gidx-2}")
                    kn = min(4, nk - k0)
                    for kk in range(kn):
                        _last = te.transpose(pk16[:, kk * 128:(kk + 1) * 128],
                                     int_sb[:, (k0 + kk) * 128:
                                            (k0 + kk + 1) * 128],
                                     s_id16[:])
                    tick("te", _last)
                    mark(f"wp_tp_{gidx}", "te")
                    gidx += 1
                mark(f"wp_te_{i}", "te")

        item_gidx = []
        _g = 0
        for (_s, _d, _j, _rl, _m) in wjobs:
            ng = (_rl // 128 + 3) // 4
            item_gidx.append((_g, ng))
            _g += ng

        @block.vector
        def _(ve):
            def emit_clip(i):
                srcj, dstj, jj, rowlen, mode = wjobs[i]
                par = i % 2
                if mode != "q64":
                    mark(f"wp_clip_{i}", "ve")
                    return
                wt(ve, f"wp_read_{i}")
                if rowlen == H and i == HT:
                    wt(ve, "dcv_prep")
                if i >= 2:
                    wt(ve, f"wp_te_{i-2}")
                t1 = wp_t1(i, rowlen)
                ve.tensor_scalar(t1, t1, RCONST + 127.0, RCONST - 128.0,
                                 ALU.min, ALU.max)
                tick("ve", ve.tensor_scalar(
                    s_hq[:, par * 4096:par * 4096 + rowlen], t1,
                    RCONST, None, ALU.subtract))
                mark(f"wp_clip_{i}", "ve")

            def emit_evac(i):
                srcj, dstj, jj, rowlen, mode = wjobs[i]
                par = i % 2
                nk = rowlen // 128
                stg = s_w2s if rowlen == H else s_w1s
                stg_off = par * rowlen
                if i >= 2:
                    wt(ve, f"wp_scr_{i-2}")
                g0, ng = item_gidx[i]
                for gg in range(ng):
                    gidx = g0 + gg
                    k0 = gg * 4
                    wt(ve, f"wp_tp_{gidx}")
                    pk16 = (pTA if gidx % 2 == 0 else pTB)[:].bitcast(BF16)
                    kn = min(4, nk - k0)
                    _last = ve.tensor_copy(
                        stg[:, stg_off + k0 * 128:stg_off + (k0 + kn) * 128],
                        pk16[:, 0:kn * 128])
                    tick("ve", _last)
                    mark(f"wp_ev_{gidx}", "ve")
                mark(f"wp_stage_{i}", "ve")

            for i in range(NJ + 1):
                if i < NJ:
                    emit_clip(i)
                if i >= 1:
                    emit_evac(i - 1)

        @block.gpsimd
        def _(gp):
            wt(gp, "dconv_stage")
            tick("scr0", gp.dma_start(out=scr_dc[:, :],
                                      in_=s_BC[:, 0:G * DCONV * 128]), 16)
            mark("dcv_prep", "scr0")
            for i, (src, dst, j, rowlen, mode) in enumerate(wjobs):
                par = i % 2
                wt(gp, f"wp_stage_{i}")
                stg = s_w2s if rowlen == H else s_w1s
                tick(f"scr{i % 2}", gp.dma_start(
                    out=dst[j, :, :],
                    in_=stg[:, par * rowlen:(par + 1) * rowlen]), 16)
                mark(f"wp_scr_{i}", f"scr{i % 2}")

        for k in SEMKEYS:
            mark(f"prep_end_{k}", k)

        st3 = s_stat.rearrange("p (q k) -> p q k", k=8)
        tg_xn = [(q, h) for q in range(NSUB) for h in range(2)]
        tg_xq = [(q, h) for q in range(NSUB) for h in range(2)]

        def xqS(cc):
            sl = cc % 2
            return s_xq[:, sl * G * TC:(sl + 1) * G * TC]

        def xnS(cc):
            sl = cc % 2
            return s_xn16[:, sl * G * (3 + TC):(sl + 1) * G * (3 + TC)]

        def xn3S(cc):
            return xnS(cc).rearrange("p (g t) -> p g t", t=3 + TC)

        def x16S(q):
            sl = q % 2
            return s_xq16[:, sl * D:(sl + 1) * D]

        def t1S(j):
            sl = j % 3
            return s_t1[:, sl * TC:(sl + 1) * TC]

        def fstgS(gi):
            return s_fstg[:, 0:512]

        MU = s_s[0:1, 0:TC]
        VCH = s_t1[0:1, 0:TC]
        MUR = s_t1[0:1, TC:2 * TC]

        def barrier(eng):
            for k in ("gp", "te", "ve", "sc", "scr0", "scr1"):
                wt(eng, f"prep_end_{k}")

        def mkpw(cc):
            def pw(eng, name):
                if cc > 0:
                    wt(eng, f"c{cc-1}_" + name)
            return pw

        def em_dum(sc, cc):
            P = f"c{cc}_"
            wt(sc, P + "x")
            for q in range(NSUB):
                sc.activation(
                    s_xqtm[:, 0:D], s_xtm[:, q * D:(q + 1) * D], AF.Copy,
                    accum_out=st3[:, q, 0:1])
                _last = sc.activation(
                    s_xqtm[:, 0:D], s_xtm[:, q * D:(q + 1) * D], AF.Square,
                    accum_out=st3[:, q, 1:2])
            tick("sc", _last)
            mark(P + "ln1_sum", "sc")
            mark(P + "ln1_sq", "sc")

        def em_ln1a(ve, cc):
            P = f"c{cc}_"

            def vst(inst):
                tick("ve", inst)
                ve.wait_ge(sems["ve"], L.cnt["ve"])

            wt(ve, P + "ln1_sq")
            ve.tensor_scalar(st3[:, :, 2:3], st3[:, :, 0:1], 1.0 / D,
                             None, ALU.mult)
            vst(ve.tensor_scalar(st3[:, :, 3:4], st3[:, :, 1:2], 1.0 / D,
                                 None, ALU.mult))
            vst(ve.tensor_tensor(st3[:, :, 4:5], st3[:, :, 2:3],
                                 st3[:, :, 2:3], ALU.mult))
            vst(ve.tensor_tensor(st3[:, :, 5:6], st3[:, :, 3:4],
                                 st3[:, :, 4:5], ALU.subtract))
            tick("ve", ve.tensor_scalar(st3[:, :, 5:6], st3[:, :, 5:6],
                                        LN_EPS, None, ALU.add))
            mark(P + "ln1_var", "ve")

        def em_ln1sd(sc, cc):
            P = f"c{cc}_"
            wt(sc, P + "ln1_var")
            tick("sc", sc.activation(st3[:, :, 5:6], st3[:, :, 5:6],
                                     AF.Sqrt))
            mark(P + "ln1_sd", "sc")

        def em_ln1b(ve, cc):
            P = f"c{cc}_"

            def vst(inst):
                tick("ve", inst)
                ve.wait_ge(sems["ve"], L.cnt["ve"])

            wt(ve, P + "ln1_sd")
            vst(ve.reciprocal(st3[:, :, 6:7], st3[:, :, 5:6]))
            vst(ve.tensor_tensor(st3[:, :, 7:8], st3[:, :, 2:3],
                                 st3[:, :, 6:7], ALU.mult))
            vst(ve.tensor_scalar(st3[:, :, 7:8], st3[:, :, 7:8],
                                 -1.0, None, ALU.mult))
            for q in range(NSUB):
                tick("ve", ve.tensor_scalar(
                    s_xtm[:, q * D:(q + 1) * D],
                    s_xtm[:, q * D:(q + 1) * D],
                    st3[:, q, 6:7], st3[:, q, 7:8],
                    ALU.mult, ALU.add))
                mark(P + f"ln1_ap_{q}", "ve")
            mark(P + "ln1_ap", "ve")

        def em_xq1(sc, cc, q):
            P = f"c{cc}_"
            wt(sc, P + f"ln1_ap_{q}")
            if q >= 1:
                wt(sc, P + f"xqclip_{q-1}")
            tick("sc", sc.activation(
                s_xqtm[:, 0:D], s_xtm[:, q * D:(q + 1) * D], AF.Copy,
                bias=RCONST, scale=16.0))
            mark(P + f"xq1_{q}", "sc")

        def em_clip(ve, cc, q):
            P = f"c{cc}_"
            wt(ve, P + f"xq1_{q}")
            if q >= 2:
                wt(ve, P + f"tp_xq_{q-2}_1")
            ve.tensor_scalar(s_xqtm[:, 0:D], s_xqtm[:, 0:D],
                             RCONST + 127.0, RCONST - 128.0,
                             ALU.min, ALU.max)
            tick("ve", ve.tensor_scalar(
                x16S(q)[:, 0:D], s_xqtm[:, 0:D], RCONST, None,
                ALU.subtract))
            mark(P + f"xqclip_{q}", "ve")

        def em_fet(te, cc, gis):
            P = f"c{cc}_"
            for gi in gis:
                pk = pBCa if gi % 2 == 0 else pBCb
                if gi >= 2:
                    wt(te, P + f"tpev_{gi-2}")
                elif cc >= 2:
                    wt(te, f"c{cc-2}_ffn_fadd_{6 + gi}")
                if gi < 8:
                    q, half = tg_xn[gi]
                    wt(te, P + f"ln1_ap_{q}")
                    for gg in range(4):
                        g = half * 4 + gg
                        _last = te.transpose(
                            pk[:, gg * 128:(gg + 1) * 128],
                            s_xtm[:, q * D + g * 128:q * D + (g + 1) * 128],
                            s_id32[:])
                    tick("te", _last)
                    mark(P + f"tp_xn_{q}_{half}", "te")
                else:
                    q, half = tg_xq[gi - 8]
                    wt(te, P + f"xqclip_{q}")
                    pk16 = pk[:].bitcast(BF16)
                    for gg in range(4):
                        g = half * 4 + gg
                        _last = te.transpose(
                            pk16[:, gg * 128:(gg + 1) * 128],
                            x16S(q)[:, g * 128:(g + 1) * 128],
                            s_id16[:])
                    tick("te", _last)
                    mark(P + f"tp_xq_{q}_{half}", "te")

        def em_fev(sc, cc, gis):
            P = f"c{cc}_"
            xn3 = xn3S(cc)
            xq3 = xqS(cc).rearrange("p (g t) -> p g t", t=TC)
            for gi in gis:
                pk = pBCa if gi % 2 == 0 else pBCb
                if gi < 8:
                    q, half = tg_xn[gi]
                    wt(sc, P + f"tp_xn_{q}_{half}")
                    dst = xn3[:, half * 4:(half + 1) * 4,
                              3 + q * 128:3 + (q + 1) * 128]
                    _last = sc.copy(
                        dst, pk[:].rearrange("p (g t) -> p g t", t=128))
                else:
                    q, half = tg_xq[gi - 8]
                    wt(sc, P + f"tp_xq_{q}_{half}")
                    dst = xq3[:, half * 4:(half + 1) * 4,
                              q * 128:(q + 1) * 128]
                    pk16 = pk[:].bitcast(BF16)
                    _last = sc.copy(
                        dst, pk16[:, 0:512].rearrange(
                            "p (g t) -> p g t", t=128))
                tick("sc", _last)
                mark(P + f"tpev_{gi}", "sc")
                if gi == 7:
                    mark(P + "xn_ready", "sc")
                if gi == 15:
                    mark(P + "xq_ready", "sc")

        def em_conv(te, cc, gs):
            P = f"c{cc}_"
            xn3 = xn3S(cc)
            for g in gs:
                pk = pBCa if g % 2 == 0 else pBCb
                if g == 0:
                    wt(te, P + "tpev_14")
                    wt(te, P + "xn_ready")
                    if cc > 0:
                        wt(te, f"c{cc-1}_halo")
                elif g == 1:
                    wt(te, P + "tpev_15")
                else:
                    wt(te, P + f"silu_{g-2}")
                if g == 0 and cc != 1:
                    wt(te, P + "dcv")
                for k in range(DCONV):
                    kk = DCONV - 1 - k
                    _last = te.matmul(
                        pk[:],
                        s_w2s[:, (g * DCONV + kk) * 128:
                              (g * DCONV + kk + 1) * 128],
                        xn3[:, g, 3 - k:3 - k + TC],
                        start=(k == 0), stop=(k == DCONV - 1))
                tick("te", _last)
                mark(P + f"cv_{g}", "te")

        def em_silu(sc, cc, gs):
            P = f"c{cc}_"
            for g in gs:
                pk = pBCa if g % 2 == 0 else pBCb
                wt(sc, P + f"cv_{g}")
                if cc > 0:
                    wt(sc, f"c{cc-1}_yg_{g}")
                tick("sc", sc.activation(
                    s_u16[:, g * TC:(g + 1) * TC], pk[:], AF.Silu,
                    bias=s_cols[:, OF_CONVB + g:OF_CONVB + g + 1]))
                mark(P + f"silu_{g}", "sc")

        def em_btct(te, cc):
            P = f"c{cc}_"
            for g in range(G):
                wt(te, P + f"silu_{g}")
                if g == 0:
                    wt(te, P + "silu_6")
                te.matmul(pBCa[0:NST, :],
                          s_BpCpT[:, g * NST:(g + 1) * NST],
                          s_u16[:, g * TC:(g + 1) * TC],
                          start=(g == 0), stop=(g == G - 1))
            for g in range(G):
                _last = te.matmul(pBCb[0:NST, :],
                                  s_BpCpT[:, (G + g) * NST:
                                          (G + g + 1) * NST],
                                  s_u16[:, g * TC:(g + 1) * TC],
                                  start=(g == 0), stop=(g == G - 1))
            tick("te", _last)
            mark(P + "btct_mm", "te")

        def em_btev(ve, cc):
            P = f"c{cc}_"
            wt(ve, P + "btct_mm")
            if cc > 0:
                wt(ve, f"c{cc-1}_bcmm_31")
            ve.tensor_scalar(s_bt[0:NST, 0:TC], pBCa[0:NST, :],
                             s_bpb[:, 0:1], None, ALU.add)
            tick("ve", ve.tensor_scalar(s_bt[0:NST, TC:2 * TC],
                                        pBCb[0:NST, :],
                                        s_bpb[:, 1:2], None, ALU.add))
            mark(P + "btct_ev", "ve")

        def em_bcmm(te, cc, gis):
            P = f"c{cc}_"
            for gi in gis:
                n, which = gi // 2, gi % 2
                pk = pBCa if gi % 2 == 0 else pBCb
                if gi < 2:
                    wt(te, P + "btct_ev")
                else:
                    wt(te, P + f"bcev_{gi-2}")
                tick("te", te.matmul(
                    pk[:], s_sel[:, n * 128:(n + 1) * 128],
                    s_bt[0:NST, which * TC:(which + 1) * TC],
                    start=True, stop=True))
                mark(P + f"bcmm_{gi}", "te")

        def em_bcev(sc, cc):
            P = f"c{cc}_"
            if cc > 0:
                wt(sc, f"c{cc-1}_z_{G * NST - 1}")
            for gi in range(2 * NST):
                n, which = gi // 2, gi % 2
                pk = pBCa if gi % 2 == 0 else pBCb
                wt(sc, P + f"bcmm_{gi}")
                tick("sc", sc.copy(
                    s_BC[:, (which * NST + n) * TC:
                         (which * NST + n + 1) * TC], pk[:]))
                mark(P + f"bcev_{gi}", "sc")

        def em_gate(te, cc):
            P = f"c{cc}_"
            for o in range(G):
                wt(te, P + f"gs_{o}")
                wt(te, P + "xn_ready")
                if o == 0:
                    wt(te, f"c{cc-1}_res_6")
                elif o == 1:
                    wt(te, f"c{cc-1}_res_7")
                else:
                    wt(te, P + f"gate_ev_{o-2}")
                pk = pMA if o % 2 == 0 else pMB
                xn3 = xn3S(cc)
                for k in range(G):
                    _last = te.matmul(
                        pk[:],
                        s_gs[:, (o % 2) * 1024 + k * 128:
                             (o % 2) * 1024 + (k + 1) * 128],
                        xn3[:, k, 3:3 + TC],
                        start=(k == 0), stop=(k == G - 1))
                tick("te", _last)
                mark(P + f"gate_mm_{o}", "te")

        def em_gateev(sc, cc):
            P = f"c{cc}_"
            pw = mkpw(cc)
            pw(sc, "out_mm_7")
            for o in range(G):
                pk = pMA if o % 2 == 0 else pMB
                wt(sc, P + f"gate_mm_{o}")
                tick("sc", sc.activation(
                    s_ga[:, o * TC:(o + 1) * TC], pk[:], AF.Silu,
                    bias=s_cols[:, OF_GATEB + o:OF_GATEB + o + 1]))
                mark(P + f"gate_ev_{o}", "sc")

        def em_f1(te, cc, j):
            P = f"c{cc}_"
            wt(te, P + f"w1_{j}")
            if j == 0:
                wt(te, P + "xq_ready")
                if cc > 0:
                    wt(te, f"c{cc-1}_ffn_h1_30")
            elif j == 1:
                if cc > 0:
                    wt(te, f"c{cc-1}_ffn_h1_31")
            else:
                wt(te, P + f"ffn_h1_{j-2}")
            bk = pTA if j % 2 == 0 else pTB
            xq = xqS(cc)
            for k in range(G):
                _last = te.matmul(bk[:],
                                  s_w1s[:, (j % 2) * 1024 + k * 128:
                                        (j % 2) * 1024 + (k + 1) * 128],
                                  xq[:, k * TC:(k + 1) * TC],
                                  start=(k == 0), stop=(k == G - 1))
            tick("te", _last)
            mark(P + f"ffn_mm1_{j}", "te")

        def em_h1(sc, cc, j):
            P = f"c{cc}_"
            if j == 0 and cc > 0:
                wt(sc, f"c{cc-1}_ln2_bc")
            wt(sc, P + f"ffn_mm1_{j}")
            bk = pTA if j % 2 == 0 else pTB
            tick("sc", sc.activation(t1S(j), bk[:], AF.Relu,
                                     bias=s_b1qC[:, j:j + 1],
                                     scale=1.0 / 64.0))
            mark(P + f"ffn_h1_{j}", "sc")

        def em_h3(sc, cc, j):
            P = f"c{cc}_"
            if j == 0 and cc > 0:
                wt(sc, f"c{cc-1}_ffn_mm2_{G-1}")
            wt(sc, P + f"gpclip_{j}")
            tick("sc", sc.activation(
                s_hq[:, j * TC:(j + 1) * TC], t1S(j), AF.Copy,
                bias=-RCONST))
            mark(P + f"ffn_h3_{j}", "sc")

        def em_faddev(sc, cc, os_):
            P = f"c{cc}_"
            for o in os_:
                pk = (pMA, pMB, pBCa, pBCb)[o % 4]
                wt(sc, P + f"ffn_mm2_{o}")
                tick("sc", sc.activation(
                    s_zres[:, o * TC:(o + 1) * TC], pk[:], AF.Copy,
                    scale=1.0 / 1024.0))
                mark(P + f"ffn_fadd_{o}", "sc")

        def mk_f2jobs(te, cc):
            P = f"c{cc}_"
            jobs = []
            for o in range(G):
                pk = (pMA, pMB, pBCa, pBCb)[o % 4]

                def mk_start(o=o, pk=pk):
                    def f():
                        wt(te, P + f"w2_{o}")
                        if o == 0:
                            if cc + 1 < NCH:
                                wt(te, f"c{cc+1}_gate_ev_6")
                            else:
                                wt(te, P + "res_6")
                            wt(te, P + f"ffn_h3_{HT-1}")
                        elif o == 1:
                            if cc + 1 < NCH:
                                wt(te, f"c{cc+1}_gate_ev_7")
                            else:
                                wt(te, P + "res_7")
                        elif o in (2, 3):
                            wt(te, P + "attn_ready")
                        else:
                            wt(te, P + f"ffn_fadd_{o-4}")
                        te.matmul(pk[:], s_w2s[:, (o % 2) * 4096:
                                                (o % 2) * 4096 + 128],
                                  s_hq[:, 0:TC], start=True, stop=False)
                    return f

                jobs.append(mk_start())
                for k in range(1, HT):
                    def mk_mid(o=o, k=k, pk=pk):
                        def f():
                            te.matmul(
                                pk[:],
                                s_w2s[:, (o % 2) * 4096 + k * 128:
                                      (o % 2) * 4096 + (k + 1) * 128],
                                s_hq[:, k * TC:(k + 1) * TC],
                                start=False, stop=False)
                        return f
                    jobs.append(mk_mid())

                def mk_seed(o=o, pk=pk):
                    def f():
                        wt(te, P + "attn_ready")
                        tick("te", te.matmul(
                            pk[:], s_id32[:],
                            s_zres[:, o * TC:(o + 1) * TC],
                            start=False, stop=True))
                        mark(P + f"ffn_mm2_{o}", "te")
                    return f
                jobs.append(mk_seed())
            return jobs

        def em_fint(te, cc, gis):
            P = f"c{cc}_"
            for gi in gis:
                q, half = gi // 2, gi % 2
                pk = pMA if gi % 2 == 0 else pMB
                if gi == 0:
                    wt(te, P + "ffn_fadd_4")
                elif gi == 1:
                    wt(te, P + "ffn_fadd_5")
                else:
                    wt(te, P + f"fin_ev_{gi-2}")
                for gg in range(4):
                    g = half * 4 + gg
                    wt(te, P + f"ffn_fadd_{g}")
                    _last = te.transpose(
                        pk[:, gg * 128:(gg + 1) * 128],
                        s_zres[:, g * TC + q * 128:g * TC + (q + 1) * 128],
                        s_id32[:])
                tick("te", _last)
                mark(P + f"fin_tp_{gi}", "te")

        def em_finev(sc, cc, gis):
            P = f"c{cc}_"
            for gi in gis:
                pk = pMA if gi % 2 == 0 else pMB
                wt(sc, P + f"fin_tp_{gi}")
                if gi == 0:
                    if cc > 0:
                        wt(sc, f"c{cc-1}_store_7")
                else:
                    wt(sc, P + f"store_{gi-1}")
                tick("sc", sc.copy(fstgS(gi), pk[:]))
                mark(P + f"fin_ev_{gi}", "sc")

        def em_store(gp, cc, gis):
            P = f"c{cc}_"
            t0 = cc * TC
            for gi in gis:
                q, half = gi // 2, gi % 2
                wt(gp, P + f"fin_ev_{gi}")
                tick(f"st{gi % 2}", gp.dma_start(
                    out=dout[t0 + q * 128:t0 + (q + 1) * 128,
                             half * 512:(half + 1) * 512],
                    in_=fstgS(gi), accum_op=ALU.add), 16)
                mark(P + f"store_{gi}", f"st{gi % 2}")

        def em_out(te, cc, o):
            P = f"c{cc}_"
            wt(te, P + f"os_{o}")
            if o == 0:
                for g in range(G):
                    wt(te, P + f"yg_{g}")
                if cc > 0:
                    wt(te, f"c{cc-1}_fin_ev_6")
            elif o == 1:
                if cc > 0:
                    wt(te, f"c{cc-1}_fin_ev_7")
            else:
                wt(te, P + f"res_{o-2}")
            pk = pMA if o % 2 == 0 else pMB
            for k in range(G):
                _last = te.matmul(
                    pk[:],
                    s_os[:, (o % 2) * 1024 + k * 128:
                         (o % 2) * 1024 + (k + 1) * 128],
                    s_ga[:, k * TC:(k + 1) * TC],
                    start=(k == 0), stop=(k == G - 1))
            tick("te", _last)
            mark(P + f"out_mm_{o}", "te")

        def em_halo(ve, cc):
            P = f"c{cc}_"
            src3 = xn3S(cc)
            dst3 = xn3S(cc + 1)
            if (cc + 1) % (NCH // BLOC) != 0:
                tick("ve", ve.tensor_copy(dst3[:, :, 0:3],
                                          src3[:, :, TC:TC + 3]))
            else:
                tick("ve", ve.memset(dst3[:, :, 0:3], 0.0))
            mark(P + "halo", "ve")

        def em_vclip(ve, cc, js):
            P = f"c{cc}_"
            for j in js:
                wt(ve, P + f"ffn_h1_{j}")
                tick("ve", ve.tensor_scalar(
                    t1S(j), t1S(j), RCONST, RCONST + 127.0,
                    ALU.add, ALU.min))
                mark(P + f"gpclip_{j}", "ve")

        def em_yg(ve, cc, g):
            P = f"c{cc}_"
            pk = pYA if g % 2 == 0 else pYB
            wt(ve, P + f"ymm_{g * NST + NST - 1}")
            wt(ve, P + f"gate_ev_{g}")
            tick("ve", ve.tensor_tensor(
                s_ga[:, g * TC:(g + 1) * TC], pk[:],
                s_ga[:, g * TC:(g + 1) * TC], ALU.mult))
            mark(P + f"yg_{g}", "ve")

        @block.sync
        def _(sy):
            tick("pf", sy.dma_start(out=dout[:], in_=dx[:]), 16)
            mark("prefill", "pf")
            barrier(sy)
            tick("dc", sy.dma_start(out=s_w2s[:, 0:G * DCONV * 128],
                                    in_=scr_dc[:, :]), 16)
            mark("c0_dcv", "dc")
            for q in range(NSUB):
                tick("x", sy.dma_start(
                    out=s_xtm[:, q * D:(q + 1) * D],
                    in_=dx[q * 128:(q + 1) * 128, :]), 16)
            mark("c0_x", "x")
            wt(sy, "c0_ln1_sq")
            wt(sy, "c0_ln1_ap")
            wt(sy, "c0_tp_xn_3_1")
            wt(sy, "c0_xq1_3")
            for q in range(NSUB):
                tick("x", sy.dma_start(
                    out=s_xtm[:, q * D:(q + 1) * D],
                    in_=dx[TC + q * 128:TC + (q + 1) * 128, :]), 16)
            mark("c1_x", "x")
            for o in range(G):
                if o >= 2:
                    wt(sy, f"c0_gate_mm_{o-2}")
                tick(f"gs{o % 2}", sy.dma_start(
                    out=s_gs[:, (o % 2) * 1024:(o % 2 + 1) * 1024],
                    in_=scr_g[o, :, :]), 16)
                mark(f"c0_gs_{o}", f"gs{o % 2}")
            for j in range(HT):
                if j >= 2:
                    wt(sy, f"c0_ffn_mm1_{j-2}")
                tick(f"w1{j % 2}", sy.dma_start(
                    out=s_w1s[:, (j % 2) * 1024:(j % 2 + 1) * 1024],
                    in_=scr_w1[j, :, :]), 16)
                mark(f"c0_w1_{j}", f"w1{j % 2}")

        @block.scalar
        def _(sc):
            barrier(sc)
            em_dum(sc, 0)

        @block.vector
        def _(ve):
            barrier(ve)
            em_ln1a(ve, 0)

        @block.scalar
        def _(sc):
            em_ln1sd(sc, 0)

        @block.vector
        def _(ve):
            em_ln1b(ve, 0)

        @block.scalar
        def _(sc):
            em_xq1(sc, 0, 0)
            em_xq1(sc, 0, 1)
            em_fev(sc, 0, range(0, 10))
            em_xq1(sc, 0, 2)
            em_fev(sc, 0, range(10, 12))
            em_xq1(sc, 0, 3)
            em_fev(sc, 0, range(12, 16))

        @block.vector
        def _(ve):
            for q in range(NSUB):
                em_clip(ve, 0, q)

        @block.tensor
        def _(te):
            barrier(te)
            em_fet(te, 0, range(16))

        @block.tensor
        def _(te):
            em_conv(te, 0, range(G))

        @block.scalar
        def _(sc):
            em_silu(sc, 0, range(G))

        @block.tensor
        def _(te):
            em_btct(te, 0)

        @block.vector
        def _(ve):
            em_btev(ve, 0)

        @block.tensor
        def _(te):
            em_bcmm(te, 0, range(2 * NST))

        @block.scalar
        def _(sc):
            em_bcev(sc, 0)

        @block.tensor
        def _(te):
            P = "c0_"
            for o in range(G):
                wt(te, P + f"gs_{o}")
                wt(te, P + "xn_ready")
                if o >= 2:
                    wt(te, P + f"gate_ev_{o-2}")
                pk = pMA if o % 2 == 0 else pMB
                xn3 = xn3S(0)
                for k in range(G):
                    _last = te.matmul(
                        pk[:],
                        s_gs[:, (o % 2) * 1024 + k * 128:
                             (o % 2) * 1024 + (k + 1) * 128],
                        xn3[:, k, 3:3 + TC],
                        start=(k == 0), stop=(k == G - 1))
                tick("te", _last)
                mark(P + f"gate_mm_{o}", "te")

        for c in range(NCH):
            P = f"c{c}_"
            P1 = f"c{c-1}_"
            nxt = c + 1 < NCH
            seq_start = (c % (NCH // BLOC) == 0)

            @block.sync
            def _(sy, c=c, P=P, nxt=nxt):
                pw = mkpw(c)
                def em_os(o):
                    if o >= 2:
                        wt(sy, P + f"out_mm_{o-2}")
                    else:
                        pw(sy, f"out_mm_{o + 6}")
                    tick(f"os{o % 2}", sy.dma_start(
                        out=s_os[:, (o % 2) * 1024:(o % 2 + 1) * 1024],
                        in_=scr_o[o, :, :]), 16)
                    mark(P + f"os_{o}", f"os{o % 2}")

                def em_w1(j):
                    Pn = f"c{c+1}_"
                    if j >= 2:
                        wt(sy, Pn + f"ffn_mm1_{j-2}")
                    else:
                        wt(sy, P + f"ffn_mm1_{j + 30}")
                    tick(f"w1{j % 2}", sy.dma_start(
                        out=s_w1s[:, (j % 2) * 1024:(j % 2 + 1) * 1024],
                        in_=scr_w1[j, :, :]), 16)
                    mark(Pn + f"w1_{j}", f"w1{j % 2}")

                def em_w2(o):
                    if o == 0 and nxt:
                        wt(sy, f"c{c+1}_cv_7")
                    if o >= 2:
                        wt(sy, P + f"ffn_mm2_{o-2}")
                    else:
                        pw(sy, f"ffn_mm2_{o + 6}")
                    tick(f"w2{o % 2}", sy.dma_start(
                        out=s_w2s[:, (o % 2) * 4096:(o % 2 + 1) * 4096],
                        in_=scr_w2[o, :, :]), 16)
                    mark(P + f"w2_{o}", f"w2{o % 2}")

                for o in range(G):
                    em_os(o)
                if nxt:
                    Pn = f"c{c+1}_"
                    for o in range(G):
                        if o >= 2:
                            wt(sy, Pn + f"gate_mm_{o-2}")
                        else:
                            wt(sy, P + f"gate_mm_{o + 6}")
                        tick(f"gs{o % 2}", sy.dma_start(
                            out=s_gs[:, (o % 2) * 1024:(o % 2 + 1) * 1024],
                            in_=scr_g[o, :, :]), 16)
                        mark(Pn + f"gs_{o}", f"gs{o % 2}")
                    em_w1(0)
                    em_w1(1)
                    for o in range(G):
                        em_w2(o)
                    for j in range(2, HT):
                        em_w1(j)
                else:
                    for o in range(G):
                        em_w2(o)

            @block.tensor
            def _(te, c=c, P=P, P1=P1, nxt=nxt):
                f2jobs = mk_f2jobs(te, c - 1) if c >= 1 else []
                ji = 0
                for g in range(G):
                    pk = pYA if g % 2 == 0 else pYB
                    if g < 2:
                        if c > 0:
                            wt(te, P1 + f"yg_{6 + g}")
                    else:
                        wt(te, P + f"yg_{g-2}")
                    te.matmul(pk[:], s_diagD[:, g * 128:(g + 1) * 128],
                              s_u16[:, g * TC:(g + 1) * TC],
                              start=True, stop=False)
                    zslots = (s_z[:, 0:TC], s_z16[:, 0:TC],
                              s_z16[:, TC:2 * TC])
                    for n in range(NST):
                        idx = g * NST + n
                        wt(te, P + f"z_{idx}")
                        tick("te", te.matmul(
                            pk[:], s_id16[:], zslots[idx % 3],
                            start=False, stop=(n == NST - 1)))
                        mark(P + f"ymm_{idx}", "te")
                        cap = 4 * (HT + 1) if g == 0 else len(f2jobs)
                        for _ in range(9):
                            if ji < min(cap, len(f2jobs)):
                                f2jobs[ji]()
                                ji += 1
                        if n == 7 and g >= 1:
                            em_f1(te, c, 4 * (g - 1) + 2)
                            em_f1(te, c, 4 * (g - 1) + 3)
                    em_f1(te, c, 4 * g)
                    em_f1(te, c, 4 * g + 1)
                    if nxt:
                        if g == 2:
                            em_fet(te, c + 1, range(0, 4))
                        elif g == 3:
                            em_fet(te, c + 1, range(4, 10))
                        elif g == 4:
                            em_fet(te, c + 1, range(10, 12))
                        elif g == 5:
                            em_fet(te, c + 1, range(12, 14))
                        elif g == 6:
                            em_fet(te, c + 1, range(14, 16))
                            em_conv(te, c + 1, range(0, 2))
                        elif g == 7:
                            em_conv(te, c + 1, range(2, 6))
                    if c >= 1:
                        if g == 3:
                            em_fint(te, c - 1, range(0, 4))
                        elif g == 4:
                            em_fint(te, c - 1, range(4, 8))
                assert ji == len(f2jobs)
                em_f1(te, c, 4 * (G - 1) + 2)
                em_f1(te, c, 4 * (G - 1) + 3)
                if nxt:
                    em_conv(te, c + 1, range(6, 8))
                    em_btct(te, c + 1)
                for o in range(G):
                    em_out(te, c, o)
                    if nxt:
                        em_bcmm(te, c + 1, range(4 * o, 4 * o + 4))
                bP = f"c{c+1}_" if nxt else P
                for g in range(G):
                    par = g % 2
                    wt(te, P + f"ln2cast_{g}")
                    if g == 0:
                        wt(te, bP + "bcev_30")
                    tick("te", te.matmul(
                        pBCa[0:1, :], s_ones16[:],
                        s_z16[:, par * TC:(par + 1) * TC],
                        start=(g == 0), stop=(g == G - 1)))
                    mark(P + f"ln2mu_{g}", "te")
                    wt(te, P + f"ln2sqc_{g}")
                    if g == 0:
                        wt(te, bP + "bcev_31")
                    tick("te", te.matmul(
                        pBCb[0:1, :], s_ones16[:],
                        s_z16[:, par * TC:(par + 1) * TC],
                        start=(g == 0), stop=(g == G - 1)))
                    mark(P + f"ln2sq_{g}", "te")
                wt(te, P + "ln2_r")
                te.matmul(pBCa[:], s_ones1[:], VCH, start=True, stop=True)
                tick("te", te.matmul(pBCb[:], s_ones1[:], MUR,
                                     start=True, stop=True))
                mark(P + "ln2_bc", "te")
                if nxt:
                    em_gate(te, c + 1)

            @block.scalar
            def _(sc, c=c, P=P, P1=P1, nxt=nxt):
                if nxt:
                    em_dum(sc, c + 1)
                em_gateev(sc, c)

                def inj(j):
                    if j == 1:
                        if nxt:
                            em_ln1sd(sc, c + 1)
                        if c >= 1:
                            em_faddev(sc, c - 1, (0, 1, 2, 3))
                    elif j == 3:
                        if nxt:
                            em_xq1(sc, c + 1, 0)
                        if c >= 1:
                            em_faddev(sc, c - 1, (4, 5))
                    elif j == 5:
                        if c >= 1:
                            em_faddev(sc, c - 1, (6, 7))
                    elif j == 7:
                        if nxt:
                            em_fev(sc, c + 1, range(0, 4))
                    elif j == 9:
                        if nxt:
                            em_xq1(sc, c + 1, 1)
                    elif j == 11:
                        if nxt:
                            em_fev(sc, c + 1, range(4, 8))
                        if c >= 1:
                            em_finev(sc, c - 1, range(0, 2))
                    elif j == 13:
                        if nxt:
                            em_fev(sc, c + 1, range(8, 10))
                            em_xq1(sc, c + 1, 2)
                        if c >= 1:
                            em_finev(sc, c - 1, range(2, 4))
                    elif j == 15:
                        if c >= 1:
                            em_finev(sc, c - 1, range(4, 8))
                    elif j == 17:
                        if nxt:
                            em_fev(sc, c + 1, range(10, 12))
                            em_xq1(sc, c + 1, 3)
                    elif j == 21:
                        if nxt:
                            em_fev(sc, c + 1, range(12, 14))
                    elif j == 25:
                        if nxt:
                            em_fev(sc, c + 1, range(14, 16))
                            em_silu(sc, c + 1, (0, 1))
                    elif j == 29:
                        if nxt:
                            em_silu(sc, c + 1, (2, 3, 4, 5, 6, 7))

                for j in range(HT):
                    if j >= 3:
                        em_h3(sc, c, j - 3)
                    em_h1(sc, c, j)
                    inj(j)
                for j in range(HT - 3, HT):
                    em_h3(sc, c, j)
                if nxt:
                    em_bcev(sc, c + 1)
                for g in range(G):
                    par = g % 2
                    wt(sc, P + f"res_{g}")
                    if g >= 2:
                        wt(sc, P + f"ln2sq_{g-2}")
                    tick("sc", sc.copy(s_z16[:, par * TC:(par + 1) * TC],
                                       s_zres[:, g * TC:(g + 1) * TC]))
                    mark(P + f"ln2cast_{g}", "sc")
                    wt(sc, P + f"ln2mu_{g}")
                    tick("sc", sc.activation(
                        s_z16[:, par * TC:(par + 1) * TC],
                        s_z16[:, par * TC:(par + 1) * TC], AF.Square))
                    mark(P + f"ln2sqc_{g}", "sc")
                wt(sc, P + "ln2_var")
                tick("sc", sc.activation(VCH, VCH, AF.Sqrt))
                mark(P + "ln2_sd", "sc")

            @block.vector
            def _(ve, c=c, P=P, P1=P1, nxt=nxt, seq_start=seq_start):
                if nxt:
                    em_halo(ve, c)
                zslots = (s_z[:, 0:TC], s_z16[:, 0:TC], s_z16[:, TC:2 * TC])
                for g in range(G):
                    for n in range(NST):
                        idx = g * NST + n
                        half = n % 2
                        zsl = zslots[idx % 3]
                        if idx >= 3:
                            wt(ve, P + f"ymm_{idx-3}")
                        if g == 0:
                            wt(ve, P + f"bcev_{2 * n}")
                            wt(ve, P + f"bcev_{2 * n + 1}")
                        ve.tensor_tensor(s_w[:],
                                         s_u16[:, g * TC:(g + 1) * TC],
                                         s_BC[:, n * TC:(n + 1) * TC],
                                         ALU.mult)
                        ci = g * NST + n
                        init = (0.0 if seq_start
                                else s_carry[:, ci:ci + 1])
                        ve.tensor_tensor_scan(
                            s_s[:, half * TC:(half + 1) * TC],
                            s_acol[:, ci:ci + 1].broadcast_to([128, TC]),
                            s_w[:], init, ALU.mult, ALU.add)
                        tick("ve", ve.tensor_tensor(
                            zsl,
                            s_s[:, half * TC:(half + 1) * TC],
                            s_BC[:, (NST + n) * TC:(NST + n + 1) * TC],
                            ALU.mult))
                        mark(P + f"z_{idx}", "ve")
                        ve.tensor_copy(
                            s_carry[:, ci:ci + 1],
                            s_s[:, (half + 1) * TC - 1:(half + 1) * TC])
                        if n == 7 and g >= 1:
                            em_vclip(ve, c, (4 * (g - 1), 4 * (g - 1) + 1))
                    if g >= 1:
                        em_yg(ve, c, g - 1)
                        em_vclip(ve, c, (4 * (g - 1) + 2, 4 * (g - 1) + 3))
                    if nxt:
                        if g == 0:
                            em_ln1a(ve, c + 1)
                        elif g == 1:
                            em_ln1b(ve, c + 1)
                        elif 2 <= g <= 5:
                            em_clip(ve, c + 1, g - 2)
                em_yg(ve, c, G - 1)
                if nxt:
                    em_btev(ve, c + 1)
                if c >= 1:
                    wt(ve, P1 + "fin_tp_7")
                xn3 = xn3S(c)
                for o in range(G):
                    pk = pMA if o % 2 == 0 else pMB
                    wt(ve, P + f"out_mm_{o}")
                    tick("ve", ve.scalar_tensor_tensor(
                        s_zres[:, o * TC:(o + 1) * TC], pk[:],
                        s_cols[:, OF_OUTB + o:OF_OUTB + o + 1],
                        xn3[:, o, 3:3 + TC],
                        ALU.add, ALU.add))
                    mark(P + f"res_{o}", "ve")
                    if o == 0:
                        em_vclip(ve, c, range(4 * (G - 1), 4 * G))

                def vst(inst):
                    tick("ve", inst)
                    ve.wait_ge(sems["ve"], L.cnt["ve"])

                wt(ve, P + f"ln2sq_{G-1}")
                wt(ve, P + "ffn_h3_30")
                wt(ve, P + "ffn_h3_31")
                vst(ve.tensor_scalar(MU, pBCa[0:1, :], 1.0 / D, None,
                                     ALU.mult))
                vst(ve.tensor_scalar(VCH, pBCb[0:1, :], 1.0 / D, None,
                                     ALU.mult))
                vst(ve.tensor_tensor(MUR, MU, MU, ALU.mult))
                vst(ve.tensor_tensor(VCH, VCH, MUR, ALU.subtract))
                tick("ve", ve.tensor_scalar(VCH, VCH, LN_EPS, None,
                                            ALU.add))
                mark(P + "ln2_var", "ve")
                wt(ve, P + "ln2_sd")
                vst(ve.reciprocal(VCH, VCH))
                tick("ve", ve.tensor_tensor(MUR, MU, VCH, ALU.mult))
                mark(P + "ln2_r", "ve")
                wt(ve, P + "ln2_bc")
                z3 = s_zres.rearrange("p (g t) -> p g t", t=TC)
                ve.tensor_tensor(z3[:, :, :], z3[:, :, :],
                                 pBCa[:, None, :].broadcast_to(
                                     [128, G, TC]), ALU.mult)
                ve.tensor_tensor(z3[:, :, :], z3[:, :, :],
                                 pBCb[:, None, :].broadcast_to(
                                     [128, G, TC]), ALU.subtract)
                for g in range(G):
                    _last = ve.tensor_scalar(
                        s_zres[:, g * TC:(g + 1) * TC],
                        s_zres[:, g * TC:(g + 1) * TC],
                        s_cols[:, OF_LNG + g:OF_LNG + g + 1],
                        s_cols[:, OF_LNB2 + g:OF_LNB2 + g + 1],
                        ALU.mult, ALU.add)
                tick("ve", _last)
                mark(P + "attn_ready", "ve")

            @block.gpsimd
            def _(gp, c=c, P=P):
                wt(gp, "prefill")
                if c >= 1 and c + 1 < NCH:
                    wt(gp, f"c{c-1}_ffn_mm2_6")
                    tick("dcg", gp.dma_start(
                        out=s_w2s[:, 0:G * DCONV * 128],
                        in_=scr_dc[:, :]), 16)
                    mark(f"c{c+1}_dcv", "dcg")
                if c >= 1:
                    em_store(gp, c - 1, range(0, 8))
                if c + 2 < NCH:
                    Pn = f"c{c+1}_"
                    wt(gp, Pn + "ln1_sq")
                    wt(gp, Pn + "ln1_ap")
                    wt(gp, Pn + "tp_xn_3_1")
                    wt(gp, Pn + "xq1_3")
                    t0n = (c + 2) * TC
                    for q in range(NSUB):
                        tick("xg", gp.dma_start(
                            out=s_xtm[:, q * D:(q + 1) * D],
                            in_=dx[t0n + q * 128:
                                   t0n + (q + 1) * 128, :]), 16)
                    mark(f"c{c+2}_x", "xg")

            for k in ("gp", "te", "ve", "sc", "st0", "st1"):
                mark(P + k, k)

        cl = NCH - 1
        Pl = f"c{cl}_"

        @block.tensor
        def _(te):
            f2jobs = mk_f2jobs(te, cl)
            for f in f2jobs:
                f()
            em_fint(te, cl, range(8))

        @block.scalar
        def _(sc):
            em_faddev(sc, cl, range(G))
            em_finev(sc, cl, range(8))

        @block.gpsimd
        def _(gp):
            em_store(gp, cl, range(8))

        if DEBUG_TAPS:
            @block.gpsimd
            def _(gp):
                for k in ("gp", "te", "ve", "sc"):
                    wt(gp, f"c{NCH-1}_" + k)
                pairs = [("xn16", s_xn16), ("u16d", s_u16), ("btd", s_bt),
                         ("BCd", s_BC), ("xqd", s_xq), ("hqd", s_hq),
                         ("gad", s_ga), ("zresd", s_zres),
                         ("carryd", s_carry), ("acold", s_acol),
                         ("t1d", s_t1), ("statd", s_stat),
                         ("xtmd", s_xtm)]
                for nm, t in pairs:
                    tick("st0", gp.dma_start(out=dbg[nm], in_=t[:]), 16)
                tick("st0", gp.dma_start(out=dbg["scrw1d"],
                                        in_=scr_w1[5, :, :]), 16)
                tick("st0", gp.dma_start(out=dbg["scrw2d"],
                                        in_=scr_w2[3, :, :]), 16)

        @block.gpsimd
        def _(gp):
            if L.prev is not None:
                for k in ("st0", "st1", "st2", "st3"):
                    gp.wait_ge(sems[k], L.cnt[k])

    return nc


def build_program():
    led1 = Ledger(None)
    nc1 = bass.Bass("TRN2", target_bir_lowering=False,
                    detect_race_conditions=False)
    emit(nc1, led1)
    led2 = Ledger(led1.marks)
    nc2 = bass.Bass("TRN2", target_bir_lowering=False,
                    detect_race_conditions=False)
    emit(nc2, led2)
    assert led1.cnt == led2.cnt, (led1.cnt, led2.cnt)
    assert led1.marks == led2.marks
    return nc2


_NC_CACHE = None


def kernel(**inputs):
    global _NC_CACHE
    if _NC_CACHE is None:
        _NC_CACHE = build_program()
    nc = _NC_CACHE

    x = np.ascontiguousarray(inputs["L0_attn_x"], dtype=np.float32)
    base = {
        "conv_w": np.ascontiguousarray(
            inputs["L0_attn_conv_w"].reshape(D, DCONV), np.float32),
        "conv_b": np.ascontiguousarray(inputs["L0_attn_conv_b"], np.float32),
        "A": np.ascontiguousarray(inputs["L0_attn_A"], np.float32),
        "Bp_w": np.ascontiguousarray(inputs["L0_attn_Bp_w"], np.float32),
        "Bp_b": np.ascontiguousarray(inputs["L0_attn_Bp_b"], np.float32),
        "Cp_w": np.ascontiguousarray(inputs["L0_attn_Cp_w"], np.float32),
        "Cp_b": np.ascontiguousarray(inputs["L0_attn_Cp_b"], np.float32),
        "Dv": np.ascontiguousarray(inputs["L0_attn_D"], np.float32),
        "gate_w": np.ascontiguousarray(inputs["L0_attn_gate_w"], np.float32),
        "gate_b": np.ascontiguousarray(inputs["L0_attn_gate_b"], np.float32),
        "out_w": np.ascontiguousarray(inputs["L0_attn_out_w"], np.float32),
        "out_b": np.ascontiguousarray(inputs["L0_attn_out_b"], np.float32),
        "ln_g": np.ascontiguousarray(inputs["L0_attn_ln_g"], np.float32),
        "ln_b": np.ascontiguousarray(inputs["L0_attn_ln_b"], np.float32),
        "w1": np.ascontiguousarray(inputs["L0_ffn_w1"], np.float32),
        "b1": np.ascontiguousarray(inputs["L0_ffn_b1"], np.float32),
        "w2": np.ascontiguousarray(inputs["L0_ffn_w2"], np.float32),
        "b2": np.ascontiguousarray(inputs["L0_ffn_b2"], np.float32),
    }
    in_maps = []
    for ci in range(NCORES):
        m = dict(base)
        m["x"] = np.ascontiguousarray(
            x[ci * BLOC:(ci + 1) * BLOC].reshape(T, D))
        in_maps.append(m)

    res = run_bass_kernel_spmd(nc, in_maps, core_ids=list(range(NCORES)))
    out = np.empty((16, SEQ, D), np.float32)
    for ci in range(NCORES):
        out[ci * BLOC:(ci + 1) * BLOC] = res.results[ci]["out"].reshape(
            BLOC, SEQ, D)
    return out
